# version 15
# baseline (speedup 1.0000x reference)
import numpy as np
from contextlib import ExitStack

import concourse.bass as bass
import concourse.mybir as mybir
from concourse.bass_utils import run_bass_kernel_spmd

F32 = mybir.dt.float32
BF16 = mybir.dt.bfloat16
I32 = mybir.dt.int32
AF = mybir.ActivationFunctionType
ALU = mybir.AluOpType

NCORE = 8
NTOK = 4096
C = 512
NCH = NTOK // C
E = 2048
ET = 16
H = 16
D = 128
M = 1024
MT = 8
I = 8192
IT = 64
EPS_LN = 1e-5
EPS_RMS = 1e-6

ENGINE_SEM = {"tensor": "pe", "scalar": "act", "vector": "dve",
              "sync": None, "gpsimd": "pool"}
DMA_SEMS = True

PARANOID = 0
DEBUG = 0
PFIX = set()
TRACE_LABELS = None


class P:

    def __init__(self, cur, eng, env):
        self.cur = cur
        self.eng = eng
        self.env = env
        self.cnt = {k: 0 for k in env.sems}
        self.marks = {}
        self.hw = {}
        self.scope = "p0"

    def mark(self, name, key):
        self.marks[name] = (key, self.cnt[key])

    def _wait_for(self, engine_name, names):
        own = ENGINE_SEM[engine_name]
        if (PARANOID == 1 or self.scope in PFIX) and self.cur == engine_name:
            for key, v in self.cnt.items():
                if key == own or v == 0:
                    continue
                hwk = (engine_name, key)
                if v > self.hw.get(hwk, 0):
                    self.hw[hwk] = v
                    self.eng.wait_ge(self.env.sems[key], v)
            return
        if PARANOID == 2 and self.cur == engine_name and own is not None \
                and self.cnt[own] > 0:
            hwk = (engine_name, own)
            v = self.cnt[own]
            if v > self.hw.get(hwk, 0):
                self.hw[hwk] = v
                self.eng.wait_ge(self.env.sems[own], v)
        for nm in names:
            if nm is None:
                continue
            key, v = self.marks[nm]
            if v == 0 or key == own:
                continue
            hwk = (engine_name, key)
            if v <= self.hw.get(hwk, 0):
                continue
            self.hw[hwk] = v
            if self.cur == engine_name:
                self.eng.wait_ge(self.env.sems[key], v)

    def _fin(self, inst, key, n, mark):
        self.cnt[key] += n
        if inst is not None:
            inst.then_inc(self.env.sems[key], n)
        if mark:
            self.marks[mark] = (key, self.cnt[key])

    def mm(self, out, lhsT, rhs, start, stop, mark=None, wait=()):
        self._wait_for("tensor", wait)
        inst = None
        if self.cur == "tensor":
            inst = self.eng.matmul(out, lhsT, rhs, start=start, stop=stop)
            if TRACE_LABELS is not None:
                TRACE_LABELS.append(self.scope)
        if mark or PARANOID or PFIX:
            self._fin(inst, "pe", 1, mark)

    def _selfsync(self, engine_name, key):
        v = self.cnt[key]
        if v == 0:
            return
        hwk = (engine_name, key)
        if v > self.hw.get(hwk, 0):
            self.hw[hwk] = v
            if self.cur == engine_name:
                self.eng.wait_ge(self.env.sems[key], v)

    def act(self, fn, *args, mark=None, wait=(), **kw):
        self._wait_for("scalar", wait)
        self._selfsync("scalar", "act")
        inst = None
        if self.cur == "scalar":
            inst = getattr(self.eng, fn)(*args, **kw)
        self._fin(inst, "act", 1, mark)

    def dve(self, fn, *args, mark=None, wait=(), **kw):
        self._wait_for("vector", wait)
        self._selfsync("vector", "dve")
        inst = None
        if self.cur == "vector":
            inst = getattr(self.eng, fn)(*args, **kw)
        self._fin(inst, "dve", 1, mark)

    def gps(self, fn, *args, mark=None, wait=(), **kw):
        self._wait_for("gpsimd", wait)
        inst = None
        if self.cur == "gpsimd":
            inst = getattr(self.eng, fn)(*args, **kw)
        if mark or PARANOID or PFIX:
            self._fin(inst, "pool", 1, mark)

    def dma(self, q, stream, out, in_, mark=None, wait=()):
        engine = "sync" if q == "wl" else "gpsimd"
        self._wait_for(engine, wait)
        inst = None
        if self.cur == engine:
            inst = self.eng.dma_start(out=out, in_=in_)
        self._fin(inst, stream, 16, mark)


class Env:
    pass


def build_nc():
    nc = bass.Bass("TRN2", target_bir_lowering=False)
    env = Env()
    env.nc = nc

    dd = {}
    dd["xT"] = nc.dram_tensor("xT", [E, NTOK], F32, kind="ExternalInput")
    dd["w_inT"] = nc.dram_tensor("w_inT", [E, E], F32, kind="ExternalInput")
    dd["w_qT"] = nc.dram_tensor("w_qT", [D, D], F32, kind="ExternalInput")
    dd["w_kT"] = nc.dram_tensor("w_kT", [D, D], F32, kind="ExternalInput")
    dd["w_vT"] = nc.dram_tensor("w_vT", [D, D], F32, kind="ExternalInput")
    dd["spT"] = nc.dram_tensor("spT", [H, D, M], F32, kind="ExternalInput")
    dd["nqw"] = nc.dram_tensor("nqw", [D, ET], F32, kind="ExternalInput")
    dd["nrw"] = nc.dram_tensor("nrw", [D, ET], F32, kind="ExternalInput")
    dd["beta"] = nc.dram_tensor("beta", [1], F32, kind="ExternalInput")
    dd["gwT"] = nc.dram_tensor("gwT", [E, I], I32, kind="ExternalInput")
    dd["uwT"] = nc.dram_tensor("uwT", [E, I], I32, kind="ExternalInput")
    dd["dwT"] = nc.dram_tensor("dwT", [I, E], I32, kind="ExternalInput")
    dd["gsc"] = nc.dram_tensor("gsc", [1, I], F32, kind="ExternalInput")
    dd["usc"] = nc.dram_tensor("usc", [1, I], F32, kind="ExternalInput")
    dd["dsc"] = nc.dram_tensor("dsc", [1, E], F32, kind="ExternalInput")
    dd["outT"] = nc.dram_tensor("outT", [E, NTOK], F32, kind="ExternalOutput")
    if DEBUG == 1:
        dd["dbg_xn"] = nc.dram_tensor("dbg_xn", [128, ET, C], BF16,
                                      kind="ExternalOutput")
        dd["dbg_rawn"] = nc.dram_tensor("dbg_rawn", [128, ET, C], BF16,
                                        kind="ExternalOutput")
        dd["dbg_s"] = nc.dram_tensor("dbg_s", [128, ET, C], BF16,
                                     kind="ExternalOutput")
        dd["dbg_g"] = nc.dram_tensor("dbg_g", [4, 128, 4, C], BF16,
                                     kind="ExternalOutput")
        dd["dbg_rqb"] = nc.dram_tensor("dbg_rqb", [128, C], F32,
                                       kind="ExternalOutput")
        dd["dbg_dv"] = nc.dram_tensor("dbg_dv", [128, 2, C], F32,
                                      kind="ExternalOutput")
    _scr = "ExternalOutput" if DEBUG == 2 else "Internal"
    dd["wg_s"] = nc.dram_tensor("wg_s", [IT, 128, ET, 128], BF16, kind=_scr)
    dd["wu_s"] = nc.dram_tensor("wu_s", [IT, 128, ET, 128], BF16, kind=_scr)
    dd["wd_s"] = nc.dram_tensor("wd_s", [4, IT, 128, 512], BF16, kind=_scr)
    dd["h_s"] = nc.dram_tensor("h_s", [IT, 128, C], BF16, kind="Internal")
    dd["wi_s"] = nc.dram_tensor("wi_s", [ET, 128, ET, 128], BF16, kind=_scr)
    dd["kk_s"] = nc.dram_tensor("kk_s", [H, D, M], BF16, kind=_scr)
    dd["kv_s"] = nc.dram_tensor("kv_s", [H, 128, MT, D], BF16, kind=_scr)
    env.d = {k: v.ap() for k, v in dd.items()}

    with ExitStack() as st:
        sb = {}

        def S(name, shape, dtype):
            sb[name] = st.enter_context(nc.sbuf_tensor(name, shape, dtype))

        S("xbf", [128, ET, C], BF16)
        S("xn", [128, ET, C], BF16)
        S("pr", [128, ET, C], BF16)
        S("probs", [128, 6, C], BF16)
        S("s_sb", [128, ET, C], BF16)
        S("hwr", [128, 2, 4, C], BF16)
        S("hrd", [128, 2, 4, C], BF16)
        S("wib", [128, 2, 2, ET * 128], BF16)
        S("kvb", [128, 2, 2048], BF16)
        S("wgb", [128, 2, ET * 128], BF16)
        S("wub", [128, 2, ET * 128], BF16)
        S("wdb", [128, 2, 4, 512], BF16)
        S("x2b", [128, 2, 2, C], F32)
        S("outb", [128, 2, 2, C], F32)
        S("qtb", [128, 2, C], BF16)
        S("sgb", [128, 2, C], BF16)
        S("gsb", [128, 4, C], BF16)
        S("tmpf", [128, 2, C], BF16)
        S("xnt", [128, C], F32)
        S("xsq", [128, 2, C], BF16)
        S("pden", [128, 2, C], BF16)
        S("rows", [1, 6, C], F32)
        S("rqb_b", [128, C], F32)
        S("r2b", [128, C], F32)
        S("r_b", [128, C], F32)
        S("mr_b", [128, C], F32)
        S("dvb", [128, 2, C], F32)
        S("ones_f", [1, 128], F32)
        S("ones_b", [128, 1], BF16)
        S("wq_b", [128, D], BF16)
        S("wk_b", [128, D], BF16)
        S("wv_b", [128, D], BF16)
        S("nqw_s", [128, ET], F32)
        S("nrw_s", [128, ET], F32)
        S("bt_s", [1, 1], F32)
        S("cst", [1, 8], F32)
        S("bsg_s", [1, 1], F32)
        S("ib0", [128, 2048], I32)
        S("ib1", [128, 2048], I32)
        env.sb = sb

        ps = {}
        ps["pA"] = st.enter_context(nc.psum_tensor("pA", [128, 1024], F32))
        ps["pB"] = st.enter_context(nc.psum_tensor("pB", [128, 1024], F32))
        ps["p4"] = st.enter_context(nc.psum_tensor("p4", [128, 512], F32))
        ps["p5"] = st.enter_context(nc.psum_tensor("p5", [128, 512], F32))
        ps["p6"] = st.enter_context(nc.psum_tensor("p6", [128, 512], F32))
        ps["p7"] = st.enter_context(nc.psum_tensor("p7", [128, 512], F32))
        env.ps = ps

        dma_streams = ["ini", "wfl0", "wfl1", "wfs0", "wfs1",
                       "kvl0", "kvl1", "kks0", "kks1", "kvs0", "kvs1",
                       "dqsc", "dql0", "dql1", "dqs0", "dqs1",
                       "wib0", "wib1", "kvk0", "kvk1", "kvv0", "kvv1",
                       "xbf", "wgl0", "wgl1", "wul0", "wul1",
                       "hwr0", "hwr1", "x20", "x21", "hrd0", "hrd1",
                       "wd0", "wd1", "wd2", "wd3", "hrd2", "out0", "out1",
                       "dbgq"]
        env.dma_streams = dma_streams
        sems = {}
        for nm in ("pe", "act", "dve", "pool") + tuple(dma_streams):
            sems[nm] = st.enter_context(nc.semaphore(nm))
        env.sems = sems

        with nc.Block() as block:
            @block.tensor
            def _(eng):
                walk(P("tensor", eng, env))

            @block.scalar
            def _(eng):
                walk(P("scalar", eng, env))

            @block.vector
            def _(eng):
                walk(P("vector", eng, env))

            @block.sync
            def _(eng):
                walk(P("sync", eng, env))

            @block.gpsimd
            def _(eng):
                walk(P("gpsimd", eng, env))

    return nc


def walk(p: P):
    env = p.env
    d = env.d
    sb = env.sb
    ps = env.ps

    pA, pB = ps["pA"], ps["pB"]
    p45 = [ps["p4"], ps["p5"]]
    p6, p7 = ps["p6"], ps["p7"]
    quarters = [pA[:, 0:512], pA[:, 512:1024], pB[:, 0:512], pB[:, 512:1024]]
    quarters2 = [ps["p4"][:, :], ps["p5"][:, :], ps["p6"][:, :], ps["p7"][:, :]]
    rows = sb["rows"]

    def row(i):
        return rows[0:1, i, :]

    def cst(i):
        return sb["cst"][0:1, i:i + 1]

    p.dma("gp", "ini", sb["wq_b"][:], d["w_qT"])
    p.dma("gp", "ini", sb["wk_b"][:], d["w_kT"])
    p.dma("gp", "ini", sb["wv_b"][:], d["w_vT"])
    p.dma("gp", "ini", sb["nqw_s"][:], d["nqw"])
    p.dma("gp", "ini", sb["nrw_s"][:], d["nrw"])
    p.dma("gp", "ini", sb["bt_s"][:], d["beta"].rearrange("(a b) -> a b", a=1),
          mark="ini_done")
    p.dve("memset", sb["ones_f"][:], 1.0, mark="c_onesf")
    p.dve("memset", sb["ones_b"][:], 1.0, mark="c_onesb")
    p.dve("memset", cst(0), EPS_LN)
    p.dve("memset", cst(1), EPS_RMS)
    p.dve("memset", cst(2), -0.5)
    p.dve("memset", cst(3), -1.0)
    p.dve("memset", cst(4), 1.0 / E, mark="c_cst")
    p.act("activation", sb["bsg_s"][:], sb["bt_s"][:], AF.Sigmoid,
          mark="sigb", wait=("ini_done",))
    p.dma("gp", "xbf", sb["xbf"].ap(),
          d["xT"][:, 0:C].rearrange("(eh el) t -> el eh t", el=128),
          mark="xbf0")

    ib = [sb["ib0"], sb["ib1"]]
    x2f = sb["x2b"].ap().rearrange("p a b t -> p (a b t)")
    obf = sb["outb"].ap().rearrange("p a b t -> p (a b t)")
    x2bf = x2f.bitcast(BF16)
    obbf = obf.bitcast(BF16)
    gstg = [x2bf[:, 0:2048].rearrange("p (eh il) -> p eh il", eh=ET),
            x2bf[:, 2048:4096].rearrange("p (eh il) -> p eh il", eh=ET)]
    ustg = [obbf[:, 0:2048].rearrange("p (eh il) -> p eh il", eh=ET),
            obbf[:, 2048:4096].rearrange("p (eh il) -> p eh il", eh=ET)]
    sgf = sb["sgb"].ap().rearrange("p a t -> p (a t)").bitcast(F32)
    tmf = sb["tmpf"].ap().rearrange("p a t -> p (a t)").bitcast(F32)
    scbg = [sgf[:, 0:128], sgf[:, 128:256]]
    scbu = [tmf[:, 0:128], tmf[:, 128:256]]
    scd = sb["hrd"].ap().rearrange("p a b t -> p (a b t)").bitcast(F32)
    ddstg = [sb["gsb"].ap().rearrange("p a t -> p (a t)"),
             sb["dvb"].ap().rearrange("p a t -> p (a t)").bitcast(BF16)]
    NIC = IT

    p7c = [None]
    row5c = [None]
    ibc = [None, None]
    dqbank = ["p7"]
    pAc = [None]

    def dq_bc_psum():
        return p7[:, :] if dqbank[0] == "p7" else pA[:, 0:512]

    def dq_bc_wait(extra=()):
        w = list(extra)
        if dqbank[0] == "p7":
            if p7c[0] is not None:
                w.append(p7c[0])
        else:
            if pAc[0] is not None:
                w.append(pAc[0])
            else:
                w.append(f"exp0h{H - 1}p2")
        return w

    def dq_bc_set(mark):
        if dqbank[0] == "p7":
            p7c[0] = mark
        else:
            pAc[0] = mark

    def p7wait(extra=()):
        w = list(extra)
        if p7c[0] is not None:
            w.append(p7c[0])
        return w

    def mk_gu_task(wi, ic):
        srcw, srcs, dst = (("gwT", "gsc", "wg_s"), ("uwT", "usc", "wu_s"))[wi]
        nm = "gu"[wi]
        stg = (gstg, ustg)[wi][ic % 2]
        scb = (scbg, scbu)[wi][ic % 2]

        def task():
            par = ic % 2
            wsc = [row5c[0]] if row5c[0] else []
            p.dma("wl", "dqsc", row(5)[:, 0:128],
                  d[srcs][:, ic * 128:(ic + 1) * 128],
                  mark=f"dq{nm}_sc{ic}", wait=wsc)
            bank = dq_bc_psum()
            p.mm(bank[:, 0:128], sb["ones_f"][:], row(5)[:, 0:128], True, True,
                 mark=f"dq{nm}_bc{ic}",
                 wait=dq_bc_wait([f"dq{nm}_sc{ic}", "c_onesf"]))
            row5c[0] = f"dq{nm}_bc{ic}"
            wcb = [f"dq{nm}_bc{ic}"]
            if ic >= 2:
                wcb.append(f"dq{nm}_dv{ic - 2}")
            p.act("copy", out=scb, in_=bank[:, 0:128],
                  mark=f"dq{nm}_cb{ic}", wait=wcb)
            dq_bc_set(f"dq{nm}_cb{ic}")
            wil = [ibc[par]] if ibc[par] else []
            p.dma("wl", f"dql{par}",
                  ib[par].rearrange("p (eh i) -> p eh i", eh=ET),
                  d[srcw][:, ic * 128:(ic + 1) * 128]
                  .rearrange("(eh el) i -> el eh i", el=128),
                  mark=f"dq{nm}_ld{ic}", wait=wil)
            wdq = [f"dq{nm}_ld{ic}", f"dq{nm}_cb{ic}"]
            if ic >= 2:
                wdq.append(f"dq{nm}_st{ic - 2}")
            for eh in range(ET):
                p.dve("scalar_tensor_tensor",
                      stg[:, eh, :],
                      ib[par][:, eh * 128:(eh + 1) * 128],
                      128.0, scb,
                      ALU.subtract, ALU.mult,
                      mark=(f"dq{nm}_dv{ic}" if eh == ET - 1 else None),
                      wait=(wdq if eh == 0 else ()))
            ibc[par] = f"dq{nm}_dv{ic}"
            p.dma("gp", f"dqs{par}", d[dst][ic],
                  stg.rearrange("p eh il -> p eh il"),
                  mark=f"dq{nm}_st{ic}", wait=(f"dq{nm}_dv{ic}",))
        return task

    def mk_dqd_task(ib_):
        def task():
            par = ib_ % 2
            if ib_ == 0:
                for q in range(4):
                    wsr = [f"dqd_pq{q - 1}"] if q >= 1 else ([row5c[0]] if row5c[0] else [])
                    p.dma("wl", "dqsc", row(5)[:, :],
                          d["dsc"][:, q * 512:(q + 1) * 512],
                          mark=f"dqd_sr{q}", wait=wsr)
                    bank = dq_bc_psum()
                    p.mm(bank[:, :], sb["ones_f"][:], row(5)[:, :], True,
                         True, mark=f"dqd_pq{q}",
                         wait=dq_bc_wait([f"dqd_sr{q}"]))
                    wcq = [f"dqd_pq{q}"]
                    p.act("copy", out=scd[:, q * 512:(q + 1) * 512],
                          in_=bank[:, :], mark=f"dqd_cq{q}", wait=wcq)
                    dq_bc_set(f"dqd_cq{q}")
                if True:
                    row5c[0] = "dqd_pq3"
            wil = [ibc[par]] if ibc[par] else []
            p.dma("wl", f"dql{par}", ib[par][:, :],
                  d["dwT"][ib_ * 128:(ib_ + 1) * 128, :],
                  mark=f"dqd_ld{ib_}", wait=wil)
            wdq = [f"dqd_ld{ib_}", "dqd_cq3"]
            if ib_ >= 2:
                wdq.append(f"dqd_st{ib_ - 2}")
            p.dve("scalar_tensor_tensor", ddstg[par], ib[par][:, :], 128.0,
                  scd, ALU.subtract, ALU.mult, mark=f"dqd_dv{ib_}", wait=wdq)
            ibc[par] = f"dqd_dv{ib_}"
            p.dma("gp", f"dqs{par}",
                  d["wd_s"][:, ib_, :, :].transpose([1, 0, 2]),
                  ddstg[par].rearrange("p (g hc) -> p g hc", g=4),
                  mark=f"dqd_st{ib_}", wait=(f"dqd_dv{ib_}",))
        return task

    dq_tasks_g = [mk_gu_task(0, ic) for ic in range(NIC)]
    dq_tasks_u = [mk_gu_task(1, ic) for ic in range(NIC)]
    dq_tasks_d = [mk_dqd_task(i) for i in range(IT)]

    def drip(lst, n):
        for _ in range(min(n, len(lst))):
            lst.pop(0)()

    wfb = [sb["outb"].ap().rearrange("p a b t -> p (a b t)"),
           sb["x2b"].ap().rearrange("p a b t -> p (a b t)")]
    wfo = [sb["wdb"][:, 0].rearrange("p a t -> p (a t)"),
           sb["wdb"][:, 1].rearrange("p a t -> p (a t)")]
    for eh in range(ET):
        par = eh % 2
        w = [f"wf_dve{eh - 2}"] if eh >= 2 else []
        p.dma("wl", f"wfl{par}", wfb[par],
              d["w_inT"][eh * 128:(eh + 1) * 128, :],
              mark=f"wf_ld{eh}", wait=w)
        p.dve("tensor_scalar", wfo[par], wfb[par],
              sb["nqw_s"][:, eh:eh + 1], None, ALU.mult,
              mark=f"wf_dve{eh}",
              wait=[f"wf_ld{eh}", "ini_done"] +
                   ([f"wf_st{eh - 2}"] if eh >= 2 else []))
        p.dma("gp", f"wfs{par}", d["wi_s"][:, :, eh, :].transpose([1, 0, 2]),
              wfo[par].rearrange("p (hdt hdl) -> p hdt hdl", hdt=ET),
              mark=f"wf_st{eh}", wait=(f"wf_dve{eh}",))
    WFOLD = ("wf_st14", "wf_st15")

    probs_flat = sb["probs"].ap().rearrange("p a t -> p (a t)")
    spbf = [probs_flat[:, 0:1024], probs_flat[:, 1024:2048]]
    kvw = [sb["wgb"][:, 0, 0:1024], sb["wgb"][:, 1, 0:1024]]
    kvv = [sb["wub"][:, 0, 0:1024], sb["wub"][:, 1, 0:1024]]
    for h in range(H):
        par = h % 2
        w = [f"kv_pe{h - 2}"] if h >= 2 else []
        p.dma("gp", f"kvl{par}", spbf[par], d["spT"][h],
              mark=f"kv_ld{h}", wait=w)
        wk = [f"kv_ld{h}", "ini_done"] + \
             ([f"kv_kc{h - 1}", f"kv_vc{h - 1}"] if h >= 1 else [])
        p.mm(pA[:, 0:512], sb["wk_b"][:], spbf[par][:, 0:512],
             True, True, wait=wk)
        p.mm(pA[:, 512:1024], sb["wk_b"][:], spbf[par][:, 512:1024],
             True, True)
        for mt in range(MT):
            p.mm(pB[:, mt * 128:(mt + 1) * 128],
                 spbf[par][:, mt * 128:(mt + 1) * 128], sb["wv_b"][:],
                 True, True, mark=(f"kv_pe{h}" if mt == MT - 1 else None))
        wc = [f"kkst{h - 2}", f"kv_st{h - 2}"] if h >= 2 else []
        p.act("copy", out=kvw[par], in_=pA[:, :], mark=f"kv_kc{h}",
              wait=[f"kv_pe{h}"] + wc)
        p.act("copy", out=kvv[par], in_=pB[:, :], mark=f"kv_vc{h}")
        p.dma("gp", f"kks{par}", d["kk_s"][h], kvw[par],
              mark=f"kkst{h}", wait=(f"kv_kc{h}",))
        p.dma("gp", f"kvs{par}", d["kv_s"][h].rearrange("p mt dd -> p (mt dd)"),
              kvv[par], mark=f"kv_st{h}", wait=(f"kv_vc{h}",))
        if h >= 2:
            drip(dq_tasks_g, 2)
    KVDONE = ("kkst14", "kkst15", "kv_st14", "kv_st15")

    def xbf_load(j):
        w = [f"xnap{j - 1}"] if j >= 1 else []
        p.dma("gp", "xbf", sb["xbf"].ap(),
              d["xT"][:, j * C:(j + 1) * C]
              .rearrange("(eh el) t -> el eh t", el=128),
              mark=f"xbf{j}", wait=w)

    def ln_stats(j, jc):
        for f in range(ET):
            if f == 0:
                w = [f"xbf{j}", f"ln_prev_free{j}", "c_onesb"]
            else:
                w = ()
            p.mm(p6[0:1, :], sb["ones_b"][:], sb["xbf"][:, f, :],
                 f == 0, f == ET - 1,
                 mark=(f"lnsx{j}" if f == ET - 1 else None), wait=w)
        p.dve("tensor_scalar", row(2), p6[0:1, :], 1.0 / E, None, ALU.mult,
              mark=f"lnm{j}", wait=(f"lnsx{j}",))
        for f in range(ET):
            par = f % 2
            wsq = [f"xbf{j}"] if f < 2 else [f"lnq_mm{j}_{f - 2}"]
            if f == 0:
                wsq.append(f"lnsq_prev{j}")
            p.act("square", out=sb["xsq"][:, par, :], in_=sb["xbf"][:, f, :],
                  mark=f"lnq_sq{j}_{f}", wait=wsq)
            p.mm(p6[0:1, :], sb["ones_b"][:], sb["xsq"][:, par, :],
                 f == 0, f == ET - 1, mark=f"lnq_mm{j}_{f}",
                 wait=(f"lnq_sq{j}_{f}",) + ((f"lnm{j}",) if f == 0 else ()))
        wt2 = [f"lnq_mm{j}_{ET - 1}"]
        if jc is not None:
            wt2.append(f"r2_r{jc}")
        p.dve("tensor_scalar", row(0), p6[0:1, :], 1.0 / E, None, ALU.mult,
              mark=f"lnt2{j}", wait=wt2)
        wmm_ = [f"r2_bc{jc}"] if jc is not None else []
        p.dve("tensor_tensor", row(1), row(2), row(2), ALU.mult,
              mark=f"lnmm{j}", wait=wmm_)
        p.dve("tensor_tensor", row(0), row(0), row(1), ALU.subtract,
              mark=f"lnvar{j}")
        p.act("activation", row(1), row(0), AF.Ln, bias=cst(0),
              mark=f"lnln{j}", wait=(f"lnvar{j}", "c_cst"))
        p.act("activation", row(0), row(1), AF.Exp, scale=cst(2),
              mark=f"lnr{j}")
        p.dve("tensor_tensor", row(1), row(2), row(0), ALU.mult,
              mark=f"lnmr{j}", wait=(f"lnr{j}",))
        p.mm(p7[:, :], sb["ones_f"][:], row(0), True, True,
             mark=f"ln_rbc{j}", wait=p7wait([f"lnr{j}"]))
        p.act("copy", out=sb["r_b"][:], in_=p7[:, :], mark=f"rb{j}",
              wait=(f"ln_rbc{j}", f"rb_free{j}"))
        p7c[0] = f"rb{j}"
        p.mm(p7[:, :], sb["ones_f"][:], row(1), True, True,
             mark=f"ln_mbc{j}", wait=p7wait([f"lnmr{j}"]))
        p.act("copy", out=sb["mr_b"][:], in_=p7[:, :], mark=f"mrb{j}",
              wait=(f"ln_mbc{j}", f"mrb_free{j}"))
        p7c[0] = f"mrb{j}"

    def ln_apply(j, prev_j):
        for f in range(ET):
            w = []
            if f == 0:
                w = [f"rb{j}", f"mrb{j}"]
                if prev_j is not None:
                    w += [f"g{prev_j}_{IT - 1}", f"u{prev_j}_{IT - 1}"]
            p.dve("tensor_tensor", sb["xnt"][:], sb["xbf"][:, f, :],
                  sb["r_b"][:], ALU.mult, wait=w)
            p.dve("tensor_tensor", sb["xn"][:, f, :], sb["xnt"][:],
                  sb["mr_b"][:], ALU.subtract,
                  mark=(f"xnap{j}" if f == ET - 1 else None))

    def load_wib_pair(j, pr):
        par = pr % 2
        if pr >= 2:
            w = [f"projmm{j}_{2 * pr - 3}"]
        elif j >= 1:
            w = [f"projmm{j - 1}_{13 + 2 * pr}"]
        else:
            w = list(WFOLD)
        p.dma("wl", f"wib{par}", sb["wib"][:, par, :, :],
              d["wi_s"][2 * pr:2 * pr + 2].transpose([1, 0, 2, 3])
              .rearrange("el a eh hdl -> el a (eh hdl)"),
              mark=f"wib{j}_p{pr}", wait=w)

    hrdbufs = [sb["hrd"][:, 0, :, :], sb["hrd"][:, 1, :, :],
               sb["ib0"].ap().bitcast(BF16)[:, 0:2048].rearrange(
                   "p (a t) -> p a t", a=4)]
    wdAbufs = [sb["wdb"][:, 0, :, :], sb["wdb"][:, 1, :, :]]
    _ib1bf = sb["ib1"].ap().bitcast(BF16)
    wdBbufs = [_ib1bf[:, 0:2048].rearrange("p (a t) -> p a t", a=4),
               _ib1bf[:, 2048:4096].rearrange("p (a t) -> p a t", a=4)]

    def _blkmark(FBx):
        jw, r = divmod(FBx, 32)
        gw, bw = divmod(r, 16)
        return f"dnb{jw}g{gw}b{bw}"

    def load_h_block(j, fb):
        g, b = divmod(fb, 16)
        FB = j * 32 + fb
        slh = FB % 3
        wfh = FB - 3
        wh = [f"hwr{j}_{b}"]
        if b >= 1:
            wh.append(f"hwr{j}_{b - 1}")
        if wfh >= 0:
            wh.append(_blkmark(wfh))
        elif slh == 2:
            wh.append(ibc[0])
        else:
            wh.append(ibc[1])
        p.dma("gp", f"hrd{slh}", hrdbufs[slh],
              d["h_s"][4 * b:4 * b + 4].transpose([1, 0, 2]),
              mark=f"hld{j}f{fb}", wait=wh)

    def load_wd_block(j, fb):
        g, b = divmod(fb, 16)
        FB = j * 32 + fb
        sld = FB % 2
        wfd = FB - 2
        wwd = []
        if wfd >= 0:
            wwd.append(_blkmark(wfd))
        else:
            wwd += ["wf_st14", "wf_st15"]
        if j == 0:
            wwd += [f"dqd_st{4 * b + 3}", f"dqd_st{4 * b + 2}"]
        p.dma("wl", f"wd{sld}", wdAbufs[sld],
              d["wd_s"][2 * g, 4 * b:4 * b + 4].transpose([1, 0, 2]),
              mark=f"wldA{j}f{fb}", wait=wwd)
        wwb = []
        if wfd >= 0:
            wwb.append(_blkmark(wfd))
        else:
            wwb.append(ibc[1])
        p.dma("wl", f"wd{2 + sld}", wdBbufs[sld],
              d["wd_s"][2 * g + 1, 4 * b:4 * b + 4].transpose([1, 0, 2]),
              mark=f"wldB{j}f{fb}", wait=wwb)

    for nm in ("ln_prev_free0", "rb_free0", "mrb_free0",
               "lnsq_prev0"):
        p.mark(nm, "pe")
    ln_stats(0, None)
    ln_apply(0, None)
    if DEBUG == 1:
        p.dma("gp", "dbgq", d["dbg_xn"], sb["xn"].ap(), wait=("xnap0",))

    for j in range(NCH):
        jm = j - 1
        p.scope = "proj"
        if j == 0:
            load_wib_pair(0, 0)
            load_wib_pair(0, 1)
        for t in range(ET):
            par = (t // 2) % 2
            q = quarters[t % 4]
            for eh in range(ET):
                wm = []
                if eh == 0:
                    wm = [f"wib{j}_p{t // 2}", f"xnap{j}"]
                    if t >= 4:
                        wm.append(f"projev{j}_{t - 4}")
                    elif j >= 1:
                        wm.append(f"gsil{jm}g1")
                    else:
                        wm += ["kv_kc15", "kv_vc15"]
                p.mm(q, sb["wib"][:, par, t % 2, eh * 128:(eh + 1) * 128],
                     sb["xn"][:, eh, :], eh == 0, eh == ET - 1,
                     mark=(f"projmm{j}_{t}" if eh == ET - 1 else None),
                     wait=wm)
            if t % 2 == 1 and 2 <= (t + 3) // 2 <= 7:
                load_wib_pair(j, (t + 3) // 2)
            if j == 0:
                drip(dq_tasks_g, 2)
            we = [f"projmm{j}_{t}"]
            if j >= 1 and t == 0:
                we.append(f"s_done{jm}")
            p.act("copy", out=sb["pr"][:, t, :], in_=q,
                  mark=f"projev{j}_{t}", wait=we)
        p.mark(f"projev_done{j}", "act")

        p.scope = "rmsq"
        for f in range(ET):
            par = f % 2
            wsq = [f"projev_done{j}"] if f < 2 else [f"rq_mm{j}_{f - 2}"]
            p.act("square", out=sb["xsq"][:, par, :], in_=sb["xn"][:, f, :],
                  mark=f"rq_sq{j}_{f}", wait=wsq)
            wmm = [f"rq_sq{j}_{f}"]
            if f == 0:
                wmm.append(f"lnt2{j}")
                if j >= 1:
                    wmm.append(f"gsil{jm}g1")
            p.mm(p6[0:1, :], sb["ones_b"][:], sb["xsq"][:, par, :],
                 f == 0, f == ET - 1, mark=f"rq_mm{j}_{f}", wait=wmm)
        p.act("activation", row(0), p6[0:1, :], AF.Ln, scale=cst(4),
              bias=cst(1), mark=f"rq_ln{j}", wait=(f"rq_mm{j}_{ET - 1}",))
        p.act("activation", row(1), row(0), AF.Exp, scale=cst(2),
              mark=f"rq_r{j}")
        p.dve("tensor_scalar", row(0), row(1), sb["bsg_s"][0:1, 0:1], None,
              ALU.mult, mark=f"rq_rb{j}", wait=(f"rq_r{j}", "sigb"))
        wbc0 = p7wait([f"rq_rb{j}"])
        if j >= 1:
            wbc0.append(f"gsil{jm}g1")
        p.mm(p7[:, :], sb["ones_f"][:], row(0), True, True,
             mark=f"rq_bc{j}", wait=wbc0)
        wrq = [f"rq_bc{j}"]
        if j >= 1:
            wrq.append(f"qtev{jm}h{H - 1}")
        p.act("copy", out=sb["rqb_b"][:], in_=p7[:, :], mark=f"rqb{j}",
              wait=wrq)
        p7c[0] = f"rqb{j}"

        p.scope = "attn"
        pending_tail = None
        for h in range(H):
            hp = h % 2
            if h >= 2:
                w = [f"at_mm{j}h{h - 2}"]
            elif j >= 1:
                w = [f"at_mm{jm}h{H - 2 + h}"]
            else:
                w = list(KVDONE)
            p.dma("wl", f"kvk{hp}", sb["kvb"][:, hp, 0:1024], d["kk_s"][h],
                  mark=f"kvk{j}h{h}", wait=w)
            p.dma("wl", f"kvv{hp}", sb["kvb"][:, hp, 1024:2048],
                  d["kv_s"][h].rearrange("p mt dd -> p (mt dd)"),
                  mark=f"kvv{j}h{h}")
            wq = [f"projev{j}_{h}", f"rqb{j}", "ini_done"]
            if h >= 2:
                wq.append(f"rawn{j}h{h - 2}")
            elif j >= 1:
                wq.append(f"rawn{jm}h{H - 2 + h}")
                wq.append(f"gsil{jm}g1")
            p.mm(p45[hp][:, :], sb["wq_b"][:], sb["pr"][:, h, :], True, True,
                 mark=f"qt{j}h{h}", wait=wq)
            p.dve("tensor_tensor", sb["qtb"][:, hp, :], p45[hp][:, :],
                  sb["rqb_b"][:], ALU.mult,
                  mark=f"qtev{j}h{h}", wait=(f"qt{j}h{h}", f"rqb{j}"))
            def emit_sc(pr_):
                pX = pA if pr_ % 2 == 0 else pB
                wsc = [f"qtev{j}h{h}", f"kvk{j}h{h}"]
                if pr_ >= 2:
                    wsc.append(f"exp{j}h{h}p{pr_ - 2}")
                elif h >= 1:
                    wsc.append(f"exp{j}h{h - 1}p{pr_ + 2}")
                else:
                    wsc.append(f"projev_done{j}")
                for half in range(2):
                    mt = pr_ * 2 + half
                    p.mm(pX[:, half * 512:(half + 1) * 512],
                         sb["kvb"][:, hp, mt * 128:(mt + 1) * 128],
                         sb["qtb"][:, hp, :], True, True,
                         mark=(f"sc{j}h{h}p{pr_}" if half == 1 else None),
                         wait=(wsc if half == 0 else ()))
                sl = (2 * pr_) % 6
                wex = [f"sc{j}h{h}p{pr_}"]
                if pr_ == 3:
                    wex += [f"at_p{j}h{h}p0", f"at_m{j}h{h}p0"]
                elif h >= 1:
                    wex.append(f"at_p{j}h{h - 1}p{pr_}")
                    wex.append(f"at_mm{j}h{h - 1}" if pr_ == 3 else
                               f"at_m{j}h{h - 1}p{pr_}")
                p.act("activation",
                      sb["probs"][:, sl:sl + 2, :]
                      .rearrange("p a t -> p (a t)"),
                      pX[:, :], AF.Exp, mark=f"exp{j}h{h}p{pr_}", wait=wex)

            def emit_attn(pr_):
                sl0 = (2 * pr_) % 6
                for half in range(2):
                    mt = pr_ * 2 + half
                    sl = sl0 + half
                    wat = [f"exp{j}h{h}p{pr_}", f"kvv{j}h{h}"]
                    if mt == 0:
                        wat.append(f"qtev{j}h{h}")
                    p.mm(p45[hp][:, :],
                         sb["kvb"][:, hp,
                                   1024 + mt * 128:1024 + (mt + 1) * 128],
                         sb["probs"][:, sl, :], mt == 0, mt == MT - 1,
                         mark=(f"at_mm{j}h{h}" if mt == MT - 1 else
                               (f"at_m{j}h{h}p{pr_}" if half == 1 else None)),
                         wait=wat)
                wpd = [f"exp{j}h{h}p{pr_}"]
                if pr_ <= 1 and h >= 1:
                    wpd.append(f"dn_mm{j}h{h - 1}")
                if pr_ != 3:
                    p.dve("tensor_tensor", sb["pden"][:, min(pr_, 1), :],
                          sb["probs"][:, sl0, :], sb["probs"][:, sl0 + 1, :],
                          ALU.add,
                          mark=f"at_p{j}h{h}p{pr_}", wait=wpd)
                else:
                    p.dve("tensor_tensor", sb["pden"][:, 1, :],
                          sb["pden"][:, 1, :], sb["probs"][:, sl0, :],
                          ALU.add, wait=wpd)
                    p.dve("tensor_tensor", sb["pden"][:, 1, :],
                          sb["pden"][:, 1, :], sb["probs"][:, sl0 + 1, :],
                          ALU.add, mark=f"at_p{j}h{h}p{pr_}")
                if pr_ == 1:
                    p.dve("tensor_tensor", sb["pden"][:, 0, :],
                          sb["pden"][:, 0, :], sb["pden"][:, 1, :],
                          ALU.add, mark=f"dn_ab{j}h{h}")

            emit_sc(0)
            emit_sc(1)
            if pending_tail is not None:
                pending_tail()
                pending_tail = None
            emit_sc(2)
            emit_attn(0)
            emit_sc(3)
            emit_attn(1)
            emit_attn(2)
            emit_attn(3)
            wd_ = [f"dn_ab{j}h{h}"]
            wd_.append(f"dn_ln{j}h{h - 1}" if h >= 1 else f"rq_ln{j}")
            p.mm(p6[0:1, :], sb["ones_b"][:], sb["pden"][:, 0, :],
                 True, False, wait=wd_)
            p.mm(p6[0:1, :], sb["ones_b"][:], sb["pden"][:, 1, :],
                 False, True, mark=f"dn_mm{j}h{h}",
                 wait=[f"at_p{j}h{h}p3"])
            p.act("activation", row(3), p6[0:1, :], AF.Ln,
                  mark=f"dn_ln{j}h{h}", wait=(f"dn_mm{j}h{h}",))
            wdi = [f"dv_bc{j}h{h - 1}"] if h >= 1 else []
            p.act("activation", row(4), row(3), AF.Exp, scale=cst(3),
                  mark=f"dn_inv{j}h{h}", wait=wdi)

            def mk_tail(h, hp):
                def tail():
                    p.mm(p7[:, :], sb["ones_f"][:], row(4), True, True,
                         mark=f"dv_bc{j}h{h}",
                         wait=p7wait([f"dn_inv{j}h{h}"]))
                    wc = [f"dv_bc{j}h{h}"]
                    if h >= 2:
                        wc.append(f"rawn{j}h{h - 2}")
                    elif j >= 1:
                        wc.append(f"rawn{jm}h{H - 2 + h}")
                        if j == 1:
                            wc.append("dqd_st63")
                    p.act("copy", out=sb["dvb"][:, hp, :], in_=p7[:, :],
                          mark=f"dv_cp{j}h{h}", wait=wc)
                    p7c[0] = f"dv_cp{j}h{h}"
                    p.dve("tensor_tensor", sb["pr"][:, h, :], p45[hp][:, :],
                          sb["dvb"][:, hp, :], ALU.mult,
                          mark=f"rawn{j}h{h}",
                          wait=(f"at_mm{j}h{h}", f"dv_cp{j}h{h}"))
                return tail

            pending_tail = mk_tail(h, hp)
            if j == 0:
                drip(dq_tasks_g, 2)
                if h >= 8:
                    drip(dq_tasks_u, 1)
            if h == 12:
                for itp in range(2):
                    if j >= 1:
                        wpre = [f"g{jm}_{IT - 2 + itp}"]
                    else:
                        wpre = [f"dqg_st{itp}", "kkst14", "kkst15"]
                    p.dma("wl", f"wgl{itp}", sb["wgb"][:, itp, :],
                          d["wg_s"][itp].rearrange("el eh il -> el (eh il)"),
                          mark=f"wgl{j}_{itp}", wait=wpre)
                    if j >= 1:
                        p.dma("wl", f"wul{itp}", sb["wub"][:, itp, :],
                              d["wu_s"][itp]
                              .rearrange("el eh il -> el (eh il)"),
                              mark=f"wul{j}_{itp}",
                              wait=[f"u{jm}_{IT - 2 + itp}"])
        if pending_tail is not None:
            pending_tail()
            pending_tail = None
        p.scope = "rms2"
        for f in range(ET):
            par = f % 2
            wsq = [f"rawn{j}h{f}"]
            if f >= 2:
                wsq.append(f"r2_mm{j}_{f - 2}")
            p.act("square", out=sb["xsq"][:, par, :], in_=sb["pr"][:, f, :],
                  mark=f"r2_sq{j}_{f}", wait=wsq)
            wmm = [f"r2_sq{j}_{f}"]
            if f == 0:
                wmm.append(f"dn_ln{j}h{H - 1}")
            p.mm(p6[0:1, :], sb["ones_b"][:], sb["xsq"][:, par, :],
                 f == 0, f == ET - 1, mark=f"r2_mm{j}_{f}", wait=wmm)
        p.act("activation", row(0), p6[0:1, :], AF.Ln, scale=cst(4),
              bias=cst(1), mark=f"r2_ln{j}", wait=(f"r2_mm{j}_{ET - 1}",))
        p.act("activation", row(1), row(0), AF.Exp, scale=cst(2),
              mark=f"r2_r{j}")
        p.mm(p7[:, :], sb["ones_f"][:], row(1), True, True,
             mark=f"r2_bc{j}", wait=p7wait([f"r2_r{j}"]))
        wr2 = [f"r2_bc{j}"]
        if j >= 1:
            wr2.append(f"s_done{jm}")
        p.act("copy", out=sb["r2b"][:], in_=p7[:, :], mark=f"r2b{j}",
              wait=wr2)
        p7c[0] = f"r2b{j}"
        for f in range(ET):
            w = [f"r2b{j}"] if f == 0 else []
            if j == 0 and f == 0:
                w.append("ini_done")
            p.dve("scalar_tensor_tensor", sb["s_sb"][:, f, :],
                  sb["pr"][:, f, :], sb["nrw_s"][:, f:f + 1], sb["r2b"][:],
                  ALU.mult, ALU.mult, wait=w)
            p.dve("tensor_tensor", sb["s_sb"][:, f, :], sb["s_sb"][:, f, :],
                  sb["xn"][:, f, :], ALU.add,
                  mark=(f"s_done{j}" if f == ET - 1 else None))
        if DEBUG == 1 and j == 0:
            p.dma("gp", "dbgq", d["dbg_rawn"], sb["pr"].ap(), wait=("s_done0",))
            p.dma("gp", "dbgq", d["dbg_s"], sb["s_sb"].ap(), wait=("s_done0",))
            p.dma("gp", "dbgq", d["dbg_rqb"], sb["rqb_b"].ap(), wait=("s_done0",))
            p.dma("gp", "dbgq", d["dbg_dv"], sb["dvb"].ap(), wait=("s_done0",))
        p.scope = "lnT"
        if j + 1 < NCH:
            xbf_load(j + 1)
            p.mark(f"ln_prev_free{j + 1}", "act")
            p.mark(f"lnsq_prev{j + 1}", "pe")
            p.mark(f"p7free{j + 1}", "act")
            p.mark(f"rb_free{j + 1}", "dve")
            p.mark(f"mrb_free{j + 1}", "dve")
            ln_stats(j + 1, j)

        p.scope = "gu"
        pg = [ps["p4"], ps["p6"]]
        pu = [ps["p5"], ps["p7"]]
        if j == 0:
            dqbank[0] = "pA"
            drip(dq_tasks_u, 12)
            for itp in range(2):
                p.dma("wl", f"wul{itp}", sb["wub"][:, itp, :],
                      d["wu_s"][itp].rearrange("el eh il -> el (eh il)"),
                      mark=f"wul{j}_{itp}",
                      wait=[f"dqu_st{itp}", "kv_st14", "kv_st15"])
        for it in range(IT):
            par = it % 2
            if j == 0:
                if it % 2 == 0:
                    drip(dq_tasks_u, 2)
                else:
                    drip(dq_tasks_d, 2)
            if it >= 2:
                wgw = [f"g{j}_{it - 2}"]
                wuw = [f"u{j}_{it - 2}"]
                if j == 0:
                    wgw.append(f"dqg_st{it}")
                    wuw.append(f"dqu_st{it}")
                p.dma("wl", f"wgl{par}", sb["wgb"][:, par, :],
                      d["wg_s"][it].rearrange("el eh il -> el (eh il)"),
                      mark=f"wgl{j}_{it}", wait=wgw)
                p.dma("wl", f"wul{par}", sb["wub"][:, par, :],
                      d["wu_s"][it].rearrange("el eh il -> el (eh il)"),
                      mark=f"wul{j}_{it}", wait=wuw)
            for eh in range(ET):
                wm = []
                if eh == 0:
                    wm = [f"wgl{j}_{it}", f"xnap{j}"]
                    if it >= 2:
                        wm.append(f"sg{j}_{it - 2}")
                    elif it == 0:
                        wm.append(f"rawn{j}h{H - 2}")
                    else:
                        wm.append(f"lnt2{j + 1}" if j + 1 < NCH
                                  else f"r2_ln{j}")
                p.mm(pg[par][:, :],
                     sb["wgb"][:, par, eh * 128:(eh + 1) * 128],
                     sb["xn"][:, eh, :], eh == 0, eh == ET - 1,
                     mark=(f"g{j}_{it}" if eh == ET - 1 else None), wait=wm)
            for eh in range(ET):
                wm = []
                if eh == 0:
                    wm = [f"wul{j}_{it}"]
                    if it >= 2:
                        wm.append(f"hv{j}_{it - 2}")
                    elif it == 0:
                        wm.append(f"rawn{j}h{H - 1}")
                        wm.append(f"qtev{j}h{H - 1}")
                    else:
                        wm.append(f"mrb{j + 1}" if j + 1 < NCH else f"r2b{j}")
                p.mm(pu[par][:, :],
                     sb["wub"][:, par, eh * 128:(eh + 1) * 128],
                     sb["xn"][:, eh, :], eh == 0, eh == ET - 1,
                     mark=(f"u{j}_{it}" if eh == ET - 1 else None), wait=wm)
            wsg = [f"g{j}_{it}"]
            if it >= 2:
                wsg.append(f"hv{j}_{it - 2}")
            elif j >= 1:
                wsg.append(f"hv{jm}_{IT - 2 + it}")
            else:
                wsg.append(f"dqg_dv{NIC - 1}")
            p.act("activation", sb["sgb"][:, par, :], pg[par][:, :], AF.Silu,
                  mark=f"sg{j}_{it}", wait=wsg)
            wb_ = it // 4
            whv = [f"u{j}_{it}", f"sg{j}_{it}"]
            if it % 4 == 0:
                if wb_ >= 2:
                    whv.append(f"hwr{j}_{wb_ - 2}")
                elif j >= 1:
                    whv.append(f"hwr{jm}_{IT // 4 - 2 + wb_}")
            p.dve("tensor_tensor", sb["hwr"][:, wb_ % 2, it % 4, :],
                  pu[par][:, :], sb["sgb"][:, par, :], ALU.mult,
                  mark=f"hv{j}_{it}", wait=whv)
            if it % 4 == 3:
                p.dma("gp", f"hwr{wb_ % 2}",
                      d["h_s"][it - 3:it + 1].transpose([1, 0, 2]),
                      sb["hwr"][:, wb_ % 2, :, :],
                      mark=f"hwr{j}_{wb_}", wait=(f"hv{j}_{it}",))
                if j >= 1:
                    if wb_ < 3:
                        load_h_block(j, wb_)
                    if wb_ < 2:
                        load_wd_block(j, wb_)

        p.scope = "down"
        if j + 1 < NCH:
            ln_apply(j + 1, j)
            load_wib_pair(j + 1, 0)
            load_wib_pair(j + 1, 1)
        if j == 0:
            drip(dq_tasks_u, 99)
            drip(dq_tasks_d, 99)
            for fb0 in range(3):
                load_h_block(0, fb0)
            for fb0 in range(2):
                load_wd_block(0, fb0)
        NB = 16
        qall = quarters + quarters2
        for g in range(2):
            for pp4 in range(2):
                pp = 4 * g + pp4
                if pp >= 2:
                    wx = [f"fin{j}p{pp - 2}"]
                elif j >= 1:
                    wx = [f"fin{jm}p{6 + pp}"]
                else:
                    wx = ["dqg_st62", "dqg_st63"]
                p.dma("gp", f"x2{pp % 2}", sb["x2b"][:, pp % 2, :, :],
                      d["xT"][256 * pp:256 * (pp + 1), j * C:(j + 1) * C]
                      .rearrange("(a el) t -> el a t", el=128),
                      mark=f"x2{j}p{pp}", wait=wx)
            for b in range(NB):
                fb = 16 * g + b
                slh = (j * 32 + fb) % 3
                sld = (j * 32 + fb) % 2
                for itl in range(4):
                    it = 4 * b + itl
                    for tt in range(8):
                        wm = []
                        if itl == 0 and tt == 0:
                            wm = [f"hld{j}f{fb}", f"wldA{j}f{fb}",
                                  f"wldB{j}f{fb}"]
                            if b == 0:
                                if g == 0:
                                    wm += [f"exp{j}h{H - 1}p2",
                                           f"exp{j}h{H - 1}p3",
                                           f"rawn{j}h{H - 1}",
                                           f"dn_ln{j}h{H - 1}",
                                           f"sg{j}_{IT - 1}",
                                           f"hv{j}_{IT - 1}"]
                                    if p7c[0] is not None:
                                        wm.append(p7c[0])
                                else:
                                    wm.append(f"gsil{j}g0")
                        wsrc = wdAbufs[sld] if tt < 4 else wdBbufs[sld]
                        p.mm(qall[tt],
                             wsrc[:, itl,
                                  (tt % 4) * 128:(tt % 4) * 128 + 128],
                             hrdbufs[slh][:, itl, :],
                             it == 0, it == IT - 1,
                             mark=(f"dnb{j}g{g}b{b}"
                                   if (itl == 3 and tt == 7) else None),
                             wait=wm)
                if fb + 3 < 32:
                    load_h_block(j, fb + 3)
                if fb + 2 < 32:
                    load_wd_block(j, fb + 2)
            for half in range(2):
                for tl in range(4):
                    tt = 4 * half + tl
                    wv = [f"dnb{j}g{g}b{NB - 1}"]
                    if tl == 0:
                        if half == 1:
                            wv.append(f"fin{j}p{4 * g + 1}")
                        elif g >= 1:
                            wv.append(f"fin{j}p3")
                        elif j >= 1:
                            wv.append(f"fin{jm}p7")
                    p.act("activation", sb["gsb"][:, tl, :], qall[tt],
                          AF.Silu,
                          mark=(f"gsil{j}g{g}h{half}"
                                if tl == 3 else None), wait=wv)
                for pq in range(2):
                    pp4 = 2 * half + pq
                    pp = 4 * g + pp4
                    f0 = 2 * pp
                    if pp4 >= 2:
                        p.dma("gp", f"x2{pp % 2}",
                              sb["x2b"][:, pp % 2, :, :],
                              d["xT"][256 * pp:256 * (pp + 1),
                                      j * C:(j + 1) * C]
                              .rearrange("(a el) t -> el a t", el=128),
                              mark=f"x2{j}p{pp}", wait=[f"fin{j}p{pp - 2}"])
                    wf = [f"gsil{j}g{g}h{half}", f"s_done{j}",
                          f"x2{j}p{pp}"]
                    if pp >= 2:
                        wf.append(f"outd{j}p{pp - 2}")
                    elif j >= 1:
                        wf.append(f"outd{jm}p{6 + pp}")
                    else:
                        wf += ["dqu_st62", "dqu_st63", f"dqu_dv{NIC - 1}"]
                    p.dve("tensor_tensor",
                          sb["tmpf"].ap().rearrange("p a t -> p (a t)"),
                          sb["s_sb"][:, f0:f0 + 2, :]
                          .rearrange("p a t -> p (a t)"),
                          sb["gsb"][:, 2 * pq:2 * pq + 2, :]
                          .rearrange("p a t -> p (a t)"),
                          ALU.mult, wait=wf)
                    p.dve("tensor_tensor",
                          sb["outb"][:, pp % 2, :, :]
                          .rearrange("p a t -> p (a t)"),
                          sb["tmpf"].ap().rearrange("p a t -> p (a t)"),
                          sb["x2b"][:, pp % 2, :, :]
                          .rearrange("p a t -> p (a t)"),
                          ALU.add, mark=f"fin{j}p{pp}")
                    p.dma("gp", f"out{pp % 2}",
                          d["outT"][128 * f0:128 * (f0 + 2),
                                    j * C:(j + 1) * C]
                          .rearrange("(a el) t -> el a t", el=128),
                          sb["outb"][:, pp % 2, :, :],
                          mark=f"outd{j}p{pp}", wait=(f"fin{j}p{pp}",))
            p.mark(f"gsil{j}g{g}", "act")
            if g == 1:
                p7c[0] = f"gsil{j}g1"

    if p.cur == "sync":
        for nm in env.dma_streams:
            if p.cnt[nm] > 0:
                p.eng.wait_ge(env.sems[nm], p.cnt[nm])



_NC_CACHE = {}


def kernel(L0_attn_query_input, L0_attn_W_in, L0_attn_W_q, L0_attn_W_k,
           L0_attn_W_v, L0_attn_stored_patterns, L0_attn_norm_query_w,
           L0_attn_norm_retrieved_w, L0_attn_beta, L0_ffn_gate_w,
           L0_ffn_gate_scale, L0_ffn_up_w, L0_ffn_up_scale, L0_ffn_down_w,
           L0_ffn_down_scale):
    x = np.asarray(L0_attn_query_input, dtype=np.float32)
    B, S, E_ = x.shape
    xT = np.ascontiguousarray(x.reshape(B * S, E_).T)

    common = dict(
        w_inT=np.ascontiguousarray(np.asarray(L0_attn_W_in, np.float32).T),
        w_qT=np.ascontiguousarray(np.asarray(L0_attn_W_q, np.float32).T),
        w_kT=np.ascontiguousarray(np.asarray(L0_attn_W_k, np.float32).T),
        w_vT=np.ascontiguousarray(np.asarray(L0_attn_W_v, np.float32).T),
        spT=np.ascontiguousarray(
            np.asarray(L0_attn_stored_patterns, np.float32).transpose(0, 2, 1)),
        nqw=np.ascontiguousarray(
            np.asarray(L0_attn_norm_query_w, np.float32).reshape(ET, 128).T),
        nrw=np.ascontiguousarray(
            np.asarray(L0_attn_norm_retrieved_w, np.float32).reshape(ET, 128).T),
        beta=np.asarray(L0_attn_beta, np.float32).reshape(1),
        gwT=np.ascontiguousarray(np.asarray(L0_ffn_gate_w, np.int32).T),
        uwT=np.ascontiguousarray(np.asarray(L0_ffn_up_w, np.int32).T),
        dwT=np.ascontiguousarray(np.asarray(L0_ffn_down_w, np.int32).T),
        gsc=np.asarray(L0_ffn_gate_scale, np.float32).reshape(1, I),
        usc=np.asarray(L0_ffn_up_scale, np.float32).reshape(1, I),
        dsc=np.asarray(L0_ffn_down_scale, np.float32).reshape(1, E),
    )

    if "nc" not in _NC_CACHE:
        _NC_CACHE["nc"] = build_nc()
    nc = _NC_CACHE["nc"]

    in_maps = []
    for c in range(NCORE):
        m = dict(common)
        m["xT"] = np.ascontiguousarray(xT[:, c * NTOK:(c + 1) * NTOK])
        in_maps.append(m)

    res = run_bass_kernel_spmd(nc, in_maps, core_ids=list(range(NCORE)))
    out = np.empty((E, B * S), dtype=np.float32)
    for c in range(NCORE):
        out[:, c * NTOK:(c + 1) * NTOK] = res.results[c]["outT"]
    return out.T.reshape(B, S, E_)


# version 16 (final)
# speedup vs baseline: 1.1039x; 1.1039x over previous
import numpy as np
from contextlib import ExitStack

import concourse.bass as bass
import concourse.mybir as mybir
from concourse.bass_utils import run_bass_kernel_spmd

F32 = mybir.dt.float32
BF16 = mybir.dt.bfloat16
I32 = mybir.dt.int32
AF = mybir.ActivationFunctionType
ALU = mybir.AluOpType

NCORE = 8
NTOK = 4096
C = 512
NCH = NTOK // C
E = 2048
ET = 16
H = 16
D = 128
M = 1024
MT = 8
I = 8192
IT = 64
EPS_LN = 1e-5
EPS_RMS = 1e-6

ENGINE_SEM = {"tensor": "pe", "scalar": "act", "vector": "dve",
              "sync": None, "gpsimd": "pool"}
DMA_SEMS = True

PARANOID = 0
DEBUG = 0
PFIX = set()
TRACE_LABELS = None


class P:

    def __init__(self, cur, eng, env):
        self.cur = cur
        self.eng = eng
        self.env = env
        self.cnt = {k: 0 for k in env.sems}
        self.marks = {}
        self.hw = {}
        self.scope = "p0"

    def mark(self, name, key):
        self.marks[name] = (key, self.cnt[key])

    def _wait_for(self, engine_name, names):
        own = ENGINE_SEM[engine_name]
        if (PARANOID == 1 or self.scope in PFIX) and self.cur == engine_name:
            for key, v in self.cnt.items():
                if key == own or v == 0:
                    continue
                hwk = (engine_name, key)
                if v > self.hw.get(hwk, 0):
                    self.hw[hwk] = v
                    self.eng.wait_ge(self.env.sems[key], v)
            return
        if PARANOID == 2 and self.cur == engine_name and own is not None \
                and self.cnt[own] > 0:
            hwk = (engine_name, own)
            v = self.cnt[own]
            if v > self.hw.get(hwk, 0):
                self.hw[hwk] = v
                self.eng.wait_ge(self.env.sems[own], v)
        for nm in names:
            if nm is None:
                continue
            key, v = self.marks[nm]
            if v == 0 or key == own:
                continue
            hwk = (engine_name, key)
            if v <= self.hw.get(hwk, 0):
                continue
            self.hw[hwk] = v
            if self.cur == engine_name:
                self.eng.wait_ge(self.env.sems[key], v)

    def _fin(self, inst, key, n, mark):
        self.cnt[key] += n
        if inst is not None:
            inst.then_inc(self.env.sems[key], n)
        if mark:
            self.marks[mark] = (key, self.cnt[key])

    def mm(self, out, lhsT, rhs, start, stop, mark=None, wait=()):
        self._wait_for("tensor", wait)
        inst = None
        if self.cur == "tensor":
            inst = self.eng.matmul(out, lhsT, rhs, start=start, stop=stop)
            if TRACE_LABELS is not None:
                TRACE_LABELS.append(self.scope)
        if mark or PARANOID or PFIX:
            self._fin(inst, "pe", 1, mark)

    def _selfsync(self, engine_name, key):
        v = self.cnt[key]
        if v == 0:
            return
        hwk = (engine_name, key)
        if v > self.hw.get(hwk, 0):
            self.hw[hwk] = v
            if self.cur == engine_name:
                self.eng.wait_ge(self.env.sems[key], v)

    def act(self, fn, *args, mark=None, wait=(), ss=False, **kw):
        self._wait_for("scalar", wait)
        if ss:
            self._selfsync("scalar", "act")
        inst = None
        if self.cur == "scalar":
            inst = getattr(self.eng, fn)(*args, **kw)
        self._fin(inst, "act", 1, mark)

    def dve(self, fn, *args, mark=None, wait=(), ss=False, **kw):
        self._wait_for("vector", wait)
        if ss:
            self._selfsync("vector", "dve")
        inst = None
        if self.cur == "vector":
            inst = getattr(self.eng, fn)(*args, **kw)
        self._fin(inst, "dve", 1, mark)

    def gps(self, fn, *args, mark=None, wait=(), **kw):
        self._wait_for("gpsimd", wait)
        inst = None
        if self.cur == "gpsimd":
            inst = getattr(self.eng, fn)(*args, **kw)
        if mark or PARANOID or PFIX:
            self._fin(inst, "pool", 1, mark)

    def dma(self, q, stream, out, in_, mark=None, wait=()):
        engine = "sync" if q == "wl" else "gpsimd"
        self._wait_for(engine, wait)
        inst = None
        if self.cur == engine:
            inst = self.eng.dma_start(out=out, in_=in_)
        self._fin(inst, stream, 16, mark)


class Env:
    pass


def build_nc():
    nc = bass.Bass("TRN2", target_bir_lowering=False)
    env = Env()
    env.nc = nc

    dd = {}
    dd["xT"] = nc.dram_tensor("xT", [E, NTOK], F32, kind="ExternalInput")
    dd["w_inT"] = nc.dram_tensor("w_inT", [E, E], F32, kind="ExternalInput")
    dd["w_qT"] = nc.dram_tensor("w_qT", [D, D], F32, kind="ExternalInput")
    dd["w_kT"] = nc.dram_tensor("w_kT", [D, D], F32, kind="ExternalInput")
    dd["w_vT"] = nc.dram_tensor("w_vT", [D, D], F32, kind="ExternalInput")
    dd["spT"] = nc.dram_tensor("spT", [H, D, M], F32, kind="ExternalInput")
    dd["nqw"] = nc.dram_tensor("nqw", [D, ET], F32, kind="ExternalInput")
    dd["nrw"] = nc.dram_tensor("nrw", [D, ET], F32, kind="ExternalInput")
    dd["beta"] = nc.dram_tensor("beta", [1], F32, kind="ExternalInput")
    dd["gwT"] = nc.dram_tensor("gwT", [E, I], I32, kind="ExternalInput")
    dd["uwT"] = nc.dram_tensor("uwT", [E, I], I32, kind="ExternalInput")
    dd["dwT"] = nc.dram_tensor("dwT", [I, E], I32, kind="ExternalInput")
    dd["gsc"] = nc.dram_tensor("gsc", [1, I], F32, kind="ExternalInput")
    dd["usc"] = nc.dram_tensor("usc", [1, I], F32, kind="ExternalInput")
    dd["dsc"] = nc.dram_tensor("dsc", [1, E], F32, kind="ExternalInput")
    dd["outT"] = nc.dram_tensor("outT", [E, NTOK], F32, kind="ExternalOutput")
    if DEBUG == 1:
        dd["dbg_xn"] = nc.dram_tensor("dbg_xn", [128, ET, C], BF16,
                                      kind="ExternalOutput")
        dd["dbg_rawn"] = nc.dram_tensor("dbg_rawn", [128, ET, C], BF16,
                                        kind="ExternalOutput")
        dd["dbg_s"] = nc.dram_tensor("dbg_s", [128, ET, C], BF16,
                                     kind="ExternalOutput")
        dd["dbg_g"] = nc.dram_tensor("dbg_g", [4, 128, 4, C], BF16,
                                     kind="ExternalOutput")
        dd["dbg_rqb"] = nc.dram_tensor("dbg_rqb", [128, C], F32,
                                       kind="ExternalOutput")
        dd["dbg_dv"] = nc.dram_tensor("dbg_dv", [128, 2, C], F32,
                                      kind="ExternalOutput")
    _scr = "ExternalOutput" if DEBUG == 2 else "Internal"
    dd["wg_s"] = nc.dram_tensor("wg_s", [IT, 128, ET, 128], BF16, kind=_scr)
    dd["wu_s"] = nc.dram_tensor("wu_s", [IT, 128, ET, 128], BF16, kind=_scr)
    dd["wd_s"] = nc.dram_tensor("wd_s", [4, IT, 128, 512], BF16, kind=_scr)
    dd["h_s"] = nc.dram_tensor("h_s", [IT, 128, C], BF16, kind="Internal")
    dd["wi_s"] = nc.dram_tensor("wi_s", [ET, 128, ET, 128], BF16, kind=_scr)
    dd["kk_s"] = nc.dram_tensor("kk_s", [H, D, M], BF16, kind=_scr)
    dd["kv_s"] = nc.dram_tensor("kv_s", [H, 128, MT, D], BF16, kind=_scr)
    env.d = {k: v.ap() for k, v in dd.items()}

    with ExitStack() as st:
        sb = {}

        def S(name, shape, dtype):
            sb[name] = st.enter_context(nc.sbuf_tensor(name, shape, dtype))

        S("xbf", [128, ET, C], BF16)
        S("xn", [128, ET, C], BF16)
        S("pr", [128, ET, C], BF16)
        S("probs", [128, 6, C], BF16)
        S("s_sb", [128, ET, C], BF16)
        S("hwr", [128, 2, 4, C], BF16)
        S("hrd", [128, 2, 4, C], BF16)
        S("wib", [128, 2, 2, ET * 128], BF16)
        S("kvb", [128, 2, 2048], BF16)
        S("wgb", [128, 2, ET * 128], BF16)
        S("wub", [128, 2, ET * 128], BF16)
        S("wdb", [128, 2, 4, 512], BF16)
        S("x2b", [128, 2, 2, C], F32)
        S("outb", [128, 2, 2, C], F32)
        S("qtb", [128, 2, C], BF16)
        S("sgb", [128, 2, C], BF16)
        S("gsb", [128, 4, C], BF16)
        S("tmpf", [128, 2, C], BF16)
        S("xnt", [128, C], F32)
        S("xsq", [128, 2, C], BF16)
        S("pden", [128, 2, C], BF16)
        S("rows", [1, 6, C], F32)
        S("rqb_b", [128, C], F32)
        S("r2b", [128, C], F32)
        S("r_b", [128, C], F32)
        S("mr_b", [128, C], F32)
        S("dvb", [128, 2, C], F32)
        S("ones_f", [1, 128], F32)
        S("ones_b", [128, 1], BF16)
        S("wq_b", [128, D], BF16)
        S("wk_b", [128, D], BF16)
        S("wv_b", [128, D], BF16)
        S("nqw_s", [128, ET], F32)
        S("nrw_s", [128, ET], F32)
        S("bt_s", [1, 1], F32)
        S("cst", [1, 8], F32)
        S("bsg_s", [1, 1], F32)
        S("ib0", [128, 2048], I32)
        S("ib1", [128, 2048], I32)
        env.sb = sb

        ps = {}
        ps["pA"] = st.enter_context(nc.psum_tensor("pA", [128, 1024], F32))
        ps["pB"] = st.enter_context(nc.psum_tensor("pB", [128, 1024], F32))
        ps["p4"] = st.enter_context(nc.psum_tensor("p4", [128, 512], F32))
        ps["p5"] = st.enter_context(nc.psum_tensor("p5", [128, 512], F32))
        ps["p6"] = st.enter_context(nc.psum_tensor("p6", [128, 512], F32))
        ps["p7"] = st.enter_context(nc.psum_tensor("p7", [128, 512], F32))
        env.ps = ps

        dma_streams = ["ini", "wfl0", "wfl1", "wfs0", "wfs1",
                       "kvl0", "kvl1", "kks0", "kks1", "kvs0", "kvs1",
                       "dqsc", "dql0", "dql1", "dqs0", "dqs1",
                       "wib0", "wib1", "kvk0", "kvk1", "kvv0", "kvv1",
                       "xbf", "wgl0", "wgl1", "wul0", "wul1",
                       "hwr0", "hwr1", "x20", "x21", "hrd0", "hrd1",
                       "wd0", "wd1", "wd2", "wd3", "hrd2", "out0", "out1",
                       "dbgq"]
        env.dma_streams = dma_streams
        sems = {}
        for nm in ("pe", "act", "dve", "pool") + tuple(dma_streams):
            sems[nm] = st.enter_context(nc.semaphore(nm))
        env.sems = sems

        with nc.Block() as block:
            @block.tensor
            def _(eng):
                walk(P("tensor", eng, env))

            @block.scalar
            def _(eng):
                walk(P("scalar", eng, env))

            @block.vector
            def _(eng):
                walk(P("vector", eng, env))

            @block.sync
            def _(eng):
                walk(P("sync", eng, env))

            @block.gpsimd
            def _(eng):
                walk(P("gpsimd", eng, env))

    return nc


def walk(p: P):
    env = p.env
    d = env.d
    sb = env.sb
    ps = env.ps

    pA, pB = ps["pA"], ps["pB"]
    p45 = [ps["p4"], ps["p5"]]
    p6, p7 = ps["p6"], ps["p7"]
    quarters = [pA[:, 0:512], pA[:, 512:1024], pB[:, 0:512], pB[:, 512:1024]]
    quarters2 = [ps["p4"][:, :], ps["p5"][:, :], ps["p6"][:, :], ps["p7"][:, :]]
    rows = sb["rows"]

    def row(i):
        return rows[0:1, i, :]

    def cst(i):
        return sb["cst"][0:1, i:i + 1]

    p.dma("gp", "ini", sb["wq_b"][:], d["w_qT"])
    p.dma("gp", "ini", sb["wk_b"][:], d["w_kT"])
    p.dma("gp", "ini", sb["wv_b"][:], d["w_vT"])
    p.dma("gp", "ini", sb["nqw_s"][:], d["nqw"])
    p.dma("gp", "ini", sb["nrw_s"][:], d["nrw"])
    p.dma("gp", "ini", sb["bt_s"][:], d["beta"].rearrange("(a b) -> a b", a=1),
          mark="ini_done")
    p.dve("memset", sb["ones_f"][:], 1.0, mark="c_onesf")
    p.dve("memset", sb["ones_b"][:], 1.0, mark="c_onesb")
    p.dve("memset", cst(0), EPS_LN)
    p.dve("memset", cst(1), EPS_RMS)
    p.dve("memset", cst(2), -0.5)
    p.dve("memset", cst(3), -1.0)
    p.dve("memset", cst(4), 1.0 / E, mark="c_cst")
    p.act("activation", sb["bsg_s"][:], sb["bt_s"][:], AF.Sigmoid,
          mark="sigb", wait=("ini_done",))
    p.dma("gp", "xbf", sb["xbf"].ap(),
          d["xT"][:, 0:C].rearrange("(eh el) t -> el eh t", el=128),
          mark="xbf0")

    ib = [sb["ib0"], sb["ib1"]]
    x2f = sb["x2b"].ap().rearrange("p a b t -> p (a b t)")
    obf = sb["outb"].ap().rearrange("p a b t -> p (a b t)")
    x2bf = x2f.bitcast(BF16)
    obbf = obf.bitcast(BF16)
    gstg = [x2bf[:, 0:2048].rearrange("p (eh il) -> p eh il", eh=ET),
            x2bf[:, 2048:4096].rearrange("p (eh il) -> p eh il", eh=ET)]
    ustg = [obbf[:, 0:2048].rearrange("p (eh il) -> p eh il", eh=ET),
            obbf[:, 2048:4096].rearrange("p (eh il) -> p eh il", eh=ET)]
    sgf = sb["sgb"].ap().rearrange("p a t -> p (a t)").bitcast(F32)
    tmf = sb["tmpf"].ap().rearrange("p a t -> p (a t)").bitcast(F32)
    scbg = [sgf[:, 0:128], sgf[:, 128:256]]
    scbu = [tmf[:, 0:128], tmf[:, 128:256]]
    scd = sb["hrd"].ap().rearrange("p a b t -> p (a b t)").bitcast(F32)
    ddstg = [sb["gsb"].ap().rearrange("p a t -> p (a t)"),
             sb["dvb"].ap().rearrange("p a t -> p (a t)").bitcast(BF16)]
    NIC = IT

    p7c = [None]
    row5c = [None]
    ibc = [None, None]
    dqbank = ["p7"]
    pAc = [None]

    def dq_bc_psum():
        return p7[:, :] if dqbank[0] == "p7" else pA[:, 0:512]

    def dq_bc_wait(extra=()):
        w = list(extra)
        if dqbank[0] == "p7":
            if p7c[0] is not None:
                w.append(p7c[0])
        else:
            if pAc[0] is not None:
                w.append(pAc[0])
            else:
                w.append(f"exp0h{H - 1}p2")
        return w

    def dq_bc_set(mark):
        if dqbank[0] == "p7":
            p7c[0] = mark
        else:
            pAc[0] = mark

    def p7wait(extra=()):
        w = list(extra)
        if p7c[0] is not None:
            w.append(p7c[0])
        return w

    def mk_gu_task(wi, ic):
        srcw, srcs, dst = (("gwT", "gsc", "wg_s"), ("uwT", "usc", "wu_s"))[wi]
        nm = "gu"[wi]
        stg = (gstg, ustg)[wi][ic % 2]
        scb = (scbg, scbu)[wi][ic % 2]

        def task():
            par = ic % 2
            wsc = [row5c[0]] if row5c[0] else []
            p.dma("wl", "dqsc", row(5)[:, 0:128],
                  d[srcs][:, ic * 128:(ic + 1) * 128],
                  mark=f"dq{nm}_sc{ic}", wait=wsc)
            bank = dq_bc_psum()
            p.mm(bank[:, 0:128], sb["ones_f"][:], row(5)[:, 0:128], True, True,
                 mark=f"dq{nm}_bc{ic}",
                 wait=dq_bc_wait([f"dq{nm}_sc{ic}", "c_onesf"]))
            row5c[0] = f"dq{nm}_bc{ic}"
            wcb = [f"dq{nm}_bc{ic}"]
            if ic >= 2:
                wcb.append(f"dq{nm}_dv{ic - 2}")
            p.act("copy", out=scb, in_=bank[:, 0:128],
                  mark=f"dq{nm}_cb{ic}", wait=wcb)
            dq_bc_set(f"dq{nm}_cb{ic}")
            wil = [ibc[par]] if ibc[par] else []
            p.dma("wl", f"dql{par}",
                  ib[par].rearrange("p (eh i) -> p eh i", eh=ET),
                  d[srcw][:, ic * 128:(ic + 1) * 128]
                  .rearrange("(eh el) i -> el eh i", el=128),
                  mark=f"dq{nm}_ld{ic}", wait=wil)
            wdq = [f"dq{nm}_ld{ic}", f"dq{nm}_cb{ic}"]
            if ic >= 2:
                wdq.append(f"dq{nm}_st{ic - 2}")
            for eh in range(ET):
                p.dve("scalar_tensor_tensor",
                      stg[:, eh, :],
                      ib[par][:, eh * 128:(eh + 1) * 128],
                      128.0, scb,
                      ALU.subtract, ALU.mult,
                      mark=(f"dq{nm}_dv{ic}" if eh == ET - 1 else None),
                      wait=(wdq if eh == 0 else ()))
            ibc[par] = f"dq{nm}_dv{ic}"
            p.dma("gp", f"dqs{par}", d[dst][ic],
                  stg.rearrange("p eh il -> p eh il"),
                  mark=f"dq{nm}_st{ic}", wait=(f"dq{nm}_dv{ic}",))
        return task

    def mk_dqd_task(ib_):
        def task():
            par = ib_ % 2
            if ib_ == 0:
                for q in range(4):
                    wsr = [f"dqd_pq{q - 1}"] if q >= 1 else ([row5c[0]] if row5c[0] else [])
                    p.dma("wl", "dqsc", row(5)[:, :],
                          d["dsc"][:, q * 512:(q + 1) * 512],
                          mark=f"dqd_sr{q}", wait=wsr)
                    bank = dq_bc_psum()
                    p.mm(bank[:, :], sb["ones_f"][:], row(5)[:, :], True,
                         True, mark=f"dqd_pq{q}",
                         wait=dq_bc_wait([f"dqd_sr{q}"]))
                    wcq = [f"dqd_pq{q}"]
                    p.act("copy", out=scd[:, q * 512:(q + 1) * 512],
                          in_=bank[:, :], mark=f"dqd_cq{q}", wait=wcq)
                    dq_bc_set(f"dqd_cq{q}")
                if True:
                    row5c[0] = "dqd_pq3"
            wil = [ibc[par]] if ibc[par] else []
            p.dma("wl", f"dql{par}", ib[par][:, :],
                  d["dwT"][ib_ * 128:(ib_ + 1) * 128, :],
                  mark=f"dqd_ld{ib_}", wait=wil)
            wdq = [f"dqd_ld{ib_}", "dqd_cq3"]
            if ib_ >= 2:
                wdq.append(f"dqd_st{ib_ - 2}")
            p.dve("scalar_tensor_tensor", ddstg[par], ib[par][:, :], 128.0,
                  scd, ALU.subtract, ALU.mult, mark=f"dqd_dv{ib_}", wait=wdq)
            ibc[par] = f"dqd_dv{ib_}"
            p.dma("gp", f"dqs{par}",
                  d["wd_s"][:, ib_, :, :].transpose([1, 0, 2]),
                  ddstg[par].rearrange("p (g hc) -> p g hc", g=4),
                  mark=f"dqd_st{ib_}", wait=(f"dqd_dv{ib_}",))
        return task

    dq_tasks_g = [mk_gu_task(0, ic) for ic in range(NIC)]
    dq_tasks_u = [mk_gu_task(1, ic) for ic in range(NIC)]
    dq_tasks_d = [mk_dqd_task(i) for i in range(IT)]

    def drip(lst, n):
        for _ in range(min(n, len(lst))):
            lst.pop(0)()

    wfb = [sb["outb"].ap().rearrange("p a b t -> p (a b t)"),
           sb["x2b"].ap().rearrange("p a b t -> p (a b t)")]
    wfo = [sb["wdb"][:, 0].rearrange("p a t -> p (a t)"),
           sb["wdb"][:, 1].rearrange("p a t -> p (a t)")]
    for eh in range(ET):
        par = eh % 2
        w = [f"wf_dve{eh - 2}"] if eh >= 2 else []
        p.dma("wl", f"wfl{par}", wfb[par],
              d["w_inT"][eh * 128:(eh + 1) * 128, :],
              mark=f"wf_ld{eh}", wait=w)
        p.dve("tensor_scalar", wfo[par], wfb[par],
              sb["nqw_s"][:, eh:eh + 1], None, ALU.mult,
              mark=f"wf_dve{eh}",
              wait=[f"wf_ld{eh}", "ini_done"] +
                   ([f"wf_st{eh - 2}"] if eh >= 2 else []))
        p.dma("gp", f"wfs{par}", d["wi_s"][:, :, eh, :].transpose([1, 0, 2]),
              wfo[par].rearrange("p (hdt hdl) -> p hdt hdl", hdt=ET),
              mark=f"wf_st{eh}", wait=(f"wf_dve{eh}",))
    WFOLD = ("wf_st14", "wf_st15")

    probs_flat = sb["probs"].ap().rearrange("p a t -> p (a t)")
    spbf = [probs_flat[:, 0:1024], probs_flat[:, 1024:2048]]
    kvw = [sb["wgb"][:, 0, 0:1024], sb["wgb"][:, 1, 0:1024]]
    kvv = [sb["wub"][:, 0, 0:1024], sb["wub"][:, 1, 0:1024]]
    for h in range(H):
        par = h % 2
        w = [f"kv_pe{h - 2}"] if h >= 2 else []
        p.dma("gp", f"kvl{par}", spbf[par], d["spT"][h],
              mark=f"kv_ld{h}", wait=w)
        wk = [f"kv_ld{h}", "ini_done"] + \
             ([f"kv_kc{h - 1}", f"kv_vc{h - 1}"] if h >= 1 else [])
        p.mm(pA[:, 0:512], sb["wk_b"][:], spbf[par][:, 0:512],
             True, True, wait=wk)
        p.mm(pA[:, 512:1024], sb["wk_b"][:], spbf[par][:, 512:1024],
             True, True)
        for mt in range(MT):
            p.mm(pB[:, mt * 128:(mt + 1) * 128],
                 spbf[par][:, mt * 128:(mt + 1) * 128], sb["wv_b"][:],
                 True, True, mark=(f"kv_pe{h}" if mt == MT - 1 else None))
        wc = [f"kkst{h - 2}", f"kv_st{h - 2}"] if h >= 2 else []
        p.act("copy", out=kvw[par], in_=pA[:, :], mark=f"kv_kc{h}",
              wait=[f"kv_pe{h}"] + wc)
        p.act("copy", out=kvv[par], in_=pB[:, :], mark=f"kv_vc{h}")
        p.dma("gp", f"kks{par}", d["kk_s"][h], kvw[par],
              mark=f"kkst{h}", wait=(f"kv_kc{h}",))
        p.dma("gp", f"kvs{par}", d["kv_s"][h].rearrange("p mt dd -> p (mt dd)"),
              kvv[par], mark=f"kv_st{h}", wait=(f"kv_vc{h}",))
        if h >= 2:
            drip(dq_tasks_g, 2)
    KVDONE = ("kkst14", "kkst15", "kv_st14", "kv_st15")

    def xbf_load(j):
        w = [f"xnap{j - 1}"] if j >= 1 else []
        p.dma("gp", "xbf", sb["xbf"].ap(),
              d["xT"][:, j * C:(j + 1) * C]
              .rearrange("(eh el) t -> el eh t", el=128),
              mark=f"xbf{j}", wait=w)

    def ln_stats(j, jc):
        for f in range(ET):
            if f == 0:
                w = [f"xbf{j}", f"ln_prev_free{j}", "c_onesb"]
            else:
                w = ()
            p.mm(p6[0:1, :], sb["ones_b"][:], sb["xbf"][:, f, :],
                 f == 0, f == ET - 1,
                 mark=(f"lnsx{j}" if f == ET - 1 else None), wait=w)
        p.dve("tensor_scalar", row(2), p6[0:1, :], 1.0 / E, None, ALU.mult,
              mark=f"lnm{j}", wait=(f"lnsx{j}",))
        for f in range(ET):
            par = f % 2
            wsq = [f"xbf{j}"] if f < 2 else [f"lnq_mm{j}_{f - 2}"]
            if f == 0:
                wsq.append(f"lnsq_prev{j}")
            p.act("square", out=sb["xsq"][:, par, :], in_=sb["xbf"][:, f, :],
                  mark=f"lnq_sq{j}_{f}", wait=wsq)
            p.mm(p6[0:1, :], sb["ones_b"][:], sb["xsq"][:, par, :],
                 f == 0, f == ET - 1, mark=f"lnq_mm{j}_{f}",
                 wait=(f"lnq_sq{j}_{f}",) + ((f"lnm{j}",) if f == 0 else ()))
        wt2 = [f"lnq_mm{j}_{ET - 1}"]
        if jc is not None:
            wt2.append(f"r2_r{jc}")
        p.dve("tensor_scalar", row(0), p6[0:1, :], 1.0 / E, None, ALU.mult,
              mark=f"lnt2{j}", wait=wt2)
        wmm_ = [f"r2_bc{jc}"] if jc is not None else []
        p.dve("tensor_tensor", row(1), row(2), row(2), ALU.mult,
              mark=f"lnmm{j}", wait=wmm_, ss=True)
        p.dve("tensor_tensor", row(0), row(0), row(1), ALU.subtract,
              mark=f"lnvar{j}", ss=True)
        p.act("activation", row(1), row(0), AF.Ln, bias=cst(0),
              mark=f"lnln{j}", wait=(f"lnvar{j}", "c_cst"))
        p.act("activation", row(0), row(1), AF.Exp, scale=cst(2),
              mark=f"lnr{j}", ss=True)
        p.dve("tensor_tensor", row(1), row(2), row(0), ALU.mult,
              mark=f"lnmr{j}", wait=(f"lnr{j}",), ss=True)
        p.mm(p7[:, :], sb["ones_f"][:], row(0), True, True,
             mark=f"ln_rbc{j}", wait=p7wait([f"lnr{j}"]))
        p.act("copy", out=sb["r_b"][:], in_=p7[:, :], mark=f"rb{j}",
              wait=(f"ln_rbc{j}", f"rb_free{j}"))
        p7c[0] = f"rb{j}"
        p.mm(p7[:, :], sb["ones_f"][:], row(1), True, True,
             mark=f"ln_mbc{j}", wait=p7wait([f"lnmr{j}"]))
        p.act("copy", out=sb["mr_b"][:], in_=p7[:, :], mark=f"mrb{j}",
              wait=(f"ln_mbc{j}", f"mrb_free{j}"))
        p7c[0] = f"mrb{j}"

    def ln_apply(j, prev_j):
        for f in range(ET):
            w = []
            if f == 0:
                w = [f"rb{j}", f"mrb{j}"]
                if prev_j is not None:
                    w += [f"g{prev_j}_{IT - 1}", f"u{prev_j}_{IT - 1}"]
            p.dve("tensor_tensor", sb["xnt"][:], sb["xbf"][:, f, :],
                  sb["r_b"][:], ALU.mult, wait=w)
            p.dve("tensor_tensor", sb["xn"][:, f, :], sb["xnt"][:],
                  sb["mr_b"][:], ALU.subtract,
                  mark=(f"xnap{j}" if f == ET - 1 else None), ss=True)

    def load_wib_pair(j, pr):
        par = pr % 2
        if pr >= 2:
            w = [f"projmm{j}_{2 * pr - 3}"]
        elif j >= 1:
            w = [f"projmm{j - 1}_{13 + 2 * pr}"]
        else:
            w = list(WFOLD)
        p.dma("wl", f"wib{par}", sb["wib"][:, par, :, :],
              d["wi_s"][2 * pr:2 * pr + 2].transpose([1, 0, 2, 3])
              .rearrange("el a eh hdl -> el a (eh hdl)"),
              mark=f"wib{j}_p{pr}", wait=w)

    hrdbufs = [sb["hrd"][:, 0, :, :], sb["hrd"][:, 1, :, :],
               sb["ib0"].ap().bitcast(BF16)[:, 0:2048].rearrange(
                   "p (a t) -> p a t", a=4)]
    wdAbufs = [sb["wdb"][:, 0, :, :], sb["wdb"][:, 1, :, :]]
    _ib1bf = sb["ib1"].ap().bitcast(BF16)
    wdBbufs = [_ib1bf[:, 0:2048].rearrange("p (a t) -> p a t", a=4),
               _ib1bf[:, 2048:4096].rearrange("p (a t) -> p a t", a=4)]

    def _blkmark(FBx):
        jw, r = divmod(FBx, 32)
        gw, bw = divmod(r, 16)
        return f"dnb{jw}g{gw}b{bw}"

    def load_h_block(j, fb):
        g, b = divmod(fb, 16)
        FB = j * 32 + fb
        slh = FB % 3
        wfh = FB - 3
        wh = [f"hwr{j}_{b}"]
        if b >= 1:
            wh.append(f"hwr{j}_{b - 1}")
        if wfh >= 0:
            wh.append(_blkmark(wfh))
        elif slh == 2:
            wh.append(ibc[0])
        else:
            wh.append(ibc[1])
        p.dma("gp", f"hrd{slh}", hrdbufs[slh],
              d["h_s"][4 * b:4 * b + 4].transpose([1, 0, 2]),
              mark=f"hld{j}f{fb}", wait=wh)

    def load_wd_block(j, fb):
        g, b = divmod(fb, 16)
        FB = j * 32 + fb
        sld = FB % 2
        wfd = FB - 2
        wwd = []
        if wfd >= 0:
            wwd.append(_blkmark(wfd))
        else:
            wwd += ["wf_st14", "wf_st15"]
        if j == 0:
            wwd += [f"dqd_st{4 * b + 3}", f"dqd_st{4 * b + 2}"]
        p.dma("wl", f"wd{sld}", wdAbufs[sld],
              d["wd_s"][2 * g, 4 * b:4 * b + 4].transpose([1, 0, 2]),
              mark=f"wldA{j}f{fb}", wait=wwd)
        wwb = []
        if wfd >= 0:
            wwb.append(_blkmark(wfd))
        else:
            wwb.append(ibc[1])
        p.dma("wl", f"wd{2 + sld}", wdBbufs[sld],
              d["wd_s"][2 * g + 1, 4 * b:4 * b + 4].transpose([1, 0, 2]),
              mark=f"wldB{j}f{fb}", wait=wwb)

    for nm in ("ln_prev_free0", "rb_free0", "mrb_free0",
               "lnsq_prev0"):
        p.mark(nm, "pe")
    ln_stats(0, None)
    ln_apply(0, None)
    if DEBUG == 1:
        p.dma("gp", "dbgq", d["dbg_xn"], sb["xn"].ap(), wait=("xnap0",))

    for j in range(NCH):
        jm = j - 1
        p.scope = "proj"
        if j == 0:
            load_wib_pair(0, 0)
            load_wib_pair(0, 1)
        for t in range(ET):
            par = (t // 2) % 2
            q = quarters[t % 4]
            for eh in range(ET):
                wm = []
                if eh == 0:
                    wm = [f"wib{j}_p{t // 2}", f"xnap{j}"]
                    if t >= 4:
                        wm.append(f"projev{j}_{t - 4}")
                    elif j >= 1:
                        wm.append(f"gsil{jm}g1")
                    else:
                        wm += ["kv_kc15", "kv_vc15"]
                p.mm(q, sb["wib"][:, par, t % 2, eh * 128:(eh + 1) * 128],
                     sb["xn"][:, eh, :], eh == 0, eh == ET - 1,
                     mark=(f"projmm{j}_{t}" if eh == ET - 1 else None),
                     wait=wm)
            if t % 2 == 1 and 2 <= (t + 3) // 2 <= 7:
                load_wib_pair(j, (t + 3) // 2)
            if j == 0:
                drip(dq_tasks_g, 2)
            we = [f"projmm{j}_{t}"]
            if j >= 1 and t == 0:
                we.append(f"s_done{jm}")
            p.act("copy", out=sb["pr"][:, t, :], in_=q,
                  mark=f"projev{j}_{t}", wait=we)
        p.mark(f"projev_done{j}", "act")

        p.scope = "rmsq"
        for f in range(ET):
            par = f % 2
            wsq = [f"projev_done{j}"] if f < 2 else [f"rq_mm{j}_{f - 2}"]
            p.act("square", out=sb["xsq"][:, par, :], in_=sb["xn"][:, f, :],
                  mark=f"rq_sq{j}_{f}", wait=wsq)
            wmm = [f"rq_sq{j}_{f}"]
            if f == 0:
                wmm.append(f"lnt2{j}")
                if j >= 1:
                    wmm.append(f"gsil{jm}g1")
            p.mm(p6[0:1, :], sb["ones_b"][:], sb["xsq"][:, par, :],
                 f == 0, f == ET - 1, mark=f"rq_mm{j}_{f}", wait=wmm)
        p.act("activation", row(0), p6[0:1, :], AF.Ln, scale=cst(4),
              bias=cst(1), mark=f"rq_ln{j}", wait=(f"rq_mm{j}_{ET - 1}",))
        p.act("activation", row(1), row(0), AF.Exp, scale=cst(2),
              mark=f"rq_r{j}", ss=True)
        p.dve("tensor_scalar", row(0), row(1), sb["bsg_s"][0:1, 0:1], None,
              ALU.mult, mark=f"rq_rb{j}", wait=(f"rq_r{j}", "sigb"))
        wbc0 = p7wait([f"rq_rb{j}"])
        if j >= 1:
            wbc0.append(f"gsil{jm}g1")
        p.mm(p7[:, :], sb["ones_f"][:], row(0), True, True,
             mark=f"rq_bc{j}", wait=wbc0)
        wrq = [f"rq_bc{j}"]
        if j >= 1:
            wrq.append(f"qtev{jm}h{H - 1}")
        p.act("copy", out=sb["rqb_b"][:], in_=p7[:, :], mark=f"rqb{j}",
              wait=wrq)
        p7c[0] = f"rqb{j}"

        p.scope = "attn"
        pending_tail = None
        for h in range(H):
            hp = h % 2
            if h >= 2:
                w = [f"at_mm{j}h{h - 2}"]
            elif j >= 1:
                w = [f"at_mm{jm}h{H - 2 + h}"]
            else:
                w = list(KVDONE)
            p.dma("wl", f"kvk{hp}", sb["kvb"][:, hp, 0:1024], d["kk_s"][h],
                  mark=f"kvk{j}h{h}", wait=w)
            p.dma("wl", f"kvv{hp}", sb["kvb"][:, hp, 1024:2048],
                  d["kv_s"][h].rearrange("p mt dd -> p (mt dd)"),
                  mark=f"kvv{j}h{h}")
            wq = [f"projev{j}_{h}", f"rqb{j}", "ini_done"]
            if h >= 2:
                wq.append(f"rawn{j}h{h - 2}")
            elif j >= 1:
                wq.append(f"rawn{jm}h{H - 2 + h}")
                wq.append(f"gsil{jm}g1")
            p.mm(p45[hp][:, :], sb["wq_b"][:], sb["pr"][:, h, :], True, True,
                 mark=f"qt{j}h{h}", wait=wq)
            p.dve("tensor_tensor", sb["qtb"][:, hp, :], p45[hp][:, :],
                  sb["rqb_b"][:], ALU.mult,
                  mark=f"qtev{j}h{h}", wait=(f"qt{j}h{h}", f"rqb{j}"))
            def emit_sc(pr_):
                pX = pA if pr_ % 2 == 0 else pB
                wsc = [f"qtev{j}h{h}", f"kvk{j}h{h}"]
                if pr_ >= 2:
                    wsc.append(f"exp{j}h{h}p{pr_ - 2}")
                elif h >= 1:
                    wsc.append(f"exp{j}h{h - 1}p{pr_ + 2}")
                else:
                    wsc.append(f"projev_done{j}")
                for half in range(2):
                    mt = pr_ * 2 + half
                    p.mm(pX[:, half * 512:(half + 1) * 512],
                         sb["kvb"][:, hp, mt * 128:(mt + 1) * 128],
                         sb["qtb"][:, hp, :], True, True,
                         mark=(f"sc{j}h{h}p{pr_}" if half == 1 else None),
                         wait=(wsc if half == 0 else ()))
                sl = (2 * pr_) % 6
                wex = [f"sc{j}h{h}p{pr_}"]
                if pr_ == 3:
                    wex += [f"at_p{j}h{h}p0", f"at_m{j}h{h}p0"]
                elif h >= 1:
                    wex.append(f"at_p{j}h{h - 1}p{pr_}")
                    wex.append(f"at_mm{j}h{h - 1}" if pr_ == 3 else
                               f"at_m{j}h{h - 1}p{pr_}")
                p.act("activation",
                      sb["probs"][:, sl:sl + 2, :]
                      .rearrange("p a t -> p (a t)"),
                      pX[:, :], AF.Exp, mark=f"exp{j}h{h}p{pr_}", wait=wex)

            def emit_attn(pr_):
                sl0 = (2 * pr_) % 6
                for half in range(2):
                    mt = pr_ * 2 + half
                    sl = sl0 + half
                    wat = [f"exp{j}h{h}p{pr_}", f"kvv{j}h{h}"]
                    if mt == 0:
                        wat.append(f"qtev{j}h{h}")
                    p.mm(p45[hp][:, :],
                         sb["kvb"][:, hp,
                                   1024 + mt * 128:1024 + (mt + 1) * 128],
                         sb["probs"][:, sl, :], mt == 0, mt == MT - 1,
                         mark=(f"at_mm{j}h{h}" if mt == MT - 1 else
                               (f"at_m{j}h{h}p{pr_}" if half == 1 else None)),
                         wait=wat)
                wpd = [f"exp{j}h{h}p{pr_}"]
                if pr_ <= 1 and h >= 1:
                    wpd.append(f"dn_mm{j}h{h - 1}")
                if pr_ != 3:
                    p.dve("tensor_tensor", sb["pden"][:, min(pr_, 1), :],
                          sb["probs"][:, sl0, :], sb["probs"][:, sl0 + 1, :],
                          ALU.add,
                          mark=f"at_p{j}h{h}p{pr_}", wait=wpd)
                else:
                    p.dve("tensor_tensor", sb["pden"][:, 1, :],
                          sb["pden"][:, 1, :], sb["probs"][:, sl0, :],
                          ALU.add, wait=wpd, ss=True)
                    p.dve("tensor_tensor", sb["pden"][:, 1, :],
                          sb["pden"][:, 1, :], sb["probs"][:, sl0 + 1, :],
                          ALU.add, mark=f"at_p{j}h{h}p{pr_}", ss=True)
                if pr_ == 1:
                    p.dve("tensor_tensor", sb["pden"][:, 0, :],
                          sb["pden"][:, 0, :], sb["pden"][:, 1, :],
                          ALU.add, mark=f"dn_ab{j}h{h}", ss=True)

            emit_sc(0)
            emit_sc(1)
            if pending_tail is not None:
                pending_tail()
                pending_tail = None
            emit_sc(2)
            emit_attn(0)
            emit_sc(3)
            emit_attn(1)
            emit_attn(2)
            emit_attn(3)
            wd_ = [f"dn_ab{j}h{h}"]
            wd_.append(f"dn_ln{j}h{h - 1}" if h >= 1 else f"rq_ln{j}")
            p.mm(p6[0:1, :], sb["ones_b"][:], sb["pden"][:, 0, :],
                 True, False, wait=wd_)
            p.mm(p6[0:1, :], sb["ones_b"][:], sb["pden"][:, 1, :],
                 False, True, mark=f"dn_mm{j}h{h}",
                 wait=[f"at_p{j}h{h}p3"])
            p.act("activation", row(3), p6[0:1, :], AF.Ln,
                  mark=f"dn_ln{j}h{h}", wait=(f"dn_mm{j}h{h}",))
            wdi = [f"dv_bc{j}h{h - 1}"] if h >= 1 else []
            p.act("activation", row(4), row(3), AF.Exp, scale=cst(3),
                  mark=f"dn_inv{j}h{h}", wait=wdi, ss=True)

            def mk_tail(h, hp):
                def tail():
                    p.mm(p7[:, :], sb["ones_f"][:], row(4), True, True,
                         mark=f"dv_bc{j}h{h}",
                         wait=p7wait([f"dn_inv{j}h{h}"]))
                    wc = [f"dv_bc{j}h{h}"]
                    if h >= 2:
                        wc.append(f"rawn{j}h{h - 2}")
                    elif j >= 1:
                        wc.append(f"rawn{jm}h{H - 2 + h}")
                        if j == 1:
                            wc.append("dqd_st63")
                    p.act("copy", out=sb["dvb"][:, hp, :], in_=p7[:, :],
                          mark=f"dv_cp{j}h{h}", wait=wc)
                    p7c[0] = f"dv_cp{j}h{h}"
                    p.dve("tensor_tensor", sb["pr"][:, h, :], p45[hp][:, :],
                          sb["dvb"][:, hp, :], ALU.mult,
                          mark=f"rawn{j}h{h}",
                          wait=(f"at_mm{j}h{h}", f"dv_cp{j}h{h}"))
                return tail

            pending_tail = mk_tail(h, hp)
            if j == 0:
                drip(dq_tasks_g, 2)
                if h >= 8:
                    drip(dq_tasks_u, 1)
            if h == 12:
                for itp in range(2):
                    if j >= 1:
                        wpre = [f"g{jm}_{IT - 2 + itp}"]
                    else:
                        wpre = [f"dqg_st{itp}", "kkst14", "kkst15"]
                    p.dma("wl", f"wgl{itp}", sb["wgb"][:, itp, :],
                          d["wg_s"][itp].rearrange("el eh il -> el (eh il)"),
                          mark=f"wgl{j}_{itp}", wait=wpre)
                    if j >= 1:
                        p.dma("wl", f"wul{itp}", sb["wub"][:, itp, :],
                              d["wu_s"][itp]
                              .rearrange("el eh il -> el (eh il)"),
                              mark=f"wul{j}_{itp}",
                              wait=[f"u{jm}_{IT - 2 + itp}"])
        if pending_tail is not None:
            pending_tail()
            pending_tail = None
        p.scope = "rms2"
        for f in range(ET):
            par = f % 2
            wsq = [f"rawn{j}h{f}"]
            if f >= 2:
                wsq.append(f"r2_mm{j}_{f - 2}")
            p.act("square", out=sb["xsq"][:, par, :], in_=sb["pr"][:, f, :],
                  mark=f"r2_sq{j}_{f}", wait=wsq)
            wmm = [f"r2_sq{j}_{f}"]
            if f == 0:
                wmm.append(f"dn_ln{j}h{H - 1}")
            p.mm(p6[0:1, :], sb["ones_b"][:], sb["xsq"][:, par, :],
                 f == 0, f == ET - 1, mark=f"r2_mm{j}_{f}", wait=wmm)
        p.act("activation", row(0), p6[0:1, :], AF.Ln, scale=cst(4),
              bias=cst(1), mark=f"r2_ln{j}", wait=(f"r2_mm{j}_{ET - 1}",))
        p.act("activation", row(1), row(0), AF.Exp, scale=cst(2),
              mark=f"r2_r{j}", ss=True)
        p.mm(p7[:, :], sb["ones_f"][:], row(1), True, True,
             mark=f"r2_bc{j}", wait=p7wait([f"r2_r{j}"]))
        wr2 = [f"r2_bc{j}"]
        if j >= 1:
            wr2.append(f"s_done{jm}")
        p.act("copy", out=sb["r2b"][:], in_=p7[:, :], mark=f"r2b{j}",
              wait=wr2)
        p7c[0] = f"r2b{j}"
        for f in range(ET):
            w = [f"r2b{j}"] if f == 0 else []
            if j == 0 and f == 0:
                w.append("ini_done")
            p.dve("scalar_tensor_tensor", sb["s_sb"][:, f, :],
                  sb["pr"][:, f, :], sb["nrw_s"][:, f:f + 1], sb["r2b"][:],
                  ALU.mult, ALU.mult, wait=w)
            p.dve("tensor_tensor", sb["s_sb"][:, f, :], sb["s_sb"][:, f, :],
                  sb["xn"][:, f, :], ALU.add,
                  mark=(f"s_done{j}" if f == ET - 1 else None), ss=True)
        if DEBUG == 1 and j == 0:
            p.dma("gp", "dbgq", d["dbg_rawn"], sb["pr"].ap(), wait=("s_done0",))
            p.dma("gp", "dbgq", d["dbg_s"], sb["s_sb"].ap(), wait=("s_done0",))
            p.dma("gp", "dbgq", d["dbg_rqb"], sb["rqb_b"].ap(), wait=("s_done0",))
            p.dma("gp", "dbgq", d["dbg_dv"], sb["dvb"].ap(), wait=("s_done0",))
        p.scope = "lnT"
        if j + 1 < NCH:
            xbf_load(j + 1)
            p.mark(f"ln_prev_free{j + 1}", "act")
            p.mark(f"lnsq_prev{j + 1}", "pe")
            p.mark(f"p7free{j + 1}", "act")
            p.mark(f"rb_free{j + 1}", "dve")
            p.mark(f"mrb_free{j + 1}", "dve")
            ln_stats(j + 1, j)

        p.scope = "gu"
        pg = [ps["p4"], ps["p6"]]
        pu = [ps["p5"], ps["p7"]]
        if j == 0:
            dqbank[0] = "pA"
            drip(dq_tasks_u, 12)
            for itp in range(2):
                p.dma("wl", f"wul{itp}", sb["wub"][:, itp, :],
                      d["wu_s"][itp].rearrange("el eh il -> el (eh il)"),
                      mark=f"wul{j}_{itp}",
                      wait=[f"dqu_st{itp}", "kv_st14", "kv_st15"])
        for it in range(IT):
            par = it % 2
            if j == 0:
                if it % 2 == 0:
                    drip(dq_tasks_u, 2)
                else:
                    drip(dq_tasks_d, 2)
            if it >= 2:
                wgw = [f"g{j}_{it - 2}"]
                wuw = [f"u{j}_{it - 2}"]
                if j == 0:
                    wgw.append(f"dqg_st{it}")
                    wuw.append(f"dqu_st{it}")
                p.dma("wl", f"wgl{par}", sb["wgb"][:, par, :],
                      d["wg_s"][it].rearrange("el eh il -> el (eh il)"),
                      mark=f"wgl{j}_{it}", wait=wgw)
                p.dma("wl", f"wul{par}", sb["wub"][:, par, :],
                      d["wu_s"][it].rearrange("el eh il -> el (eh il)"),
                      mark=f"wul{j}_{it}", wait=wuw)
            for eh in range(ET):
                wm = []
                if eh == 0:
                    wm = [f"wgl{j}_{it}", f"xnap{j}"]
                    if it >= 2:
                        wm.append(f"sg{j}_{it - 2}")
                    elif it == 0:
                        wm.append(f"rawn{j}h{H - 2}")
                    else:
                        wm.append(f"lnt2{j + 1}" if j + 1 < NCH
                                  else f"r2_ln{j}")
                p.mm(pg[par][:, :],
                     sb["wgb"][:, par, eh * 128:(eh + 1) * 128],
                     sb["xn"][:, eh, :], eh == 0, eh == ET - 1,
                     mark=(f"g{j}_{it}" if eh == ET - 1 else None), wait=wm)
            for eh in range(ET):
                wm = []
                if eh == 0:
                    wm = [f"wul{j}_{it}"]
                    if it >= 2:
                        wm.append(f"hv{j}_{it - 2}")
                    elif it == 0:
                        wm.append(f"rawn{j}h{H - 1}")
                        wm.append(f"qtev{j}h{H - 1}")
                    else:
                        wm.append(f"mrb{j + 1}" if j + 1 < NCH else f"r2b{j}")
                p.mm(pu[par][:, :],
                     sb["wub"][:, par, eh * 128:(eh + 1) * 128],
                     sb["xn"][:, eh, :], eh == 0, eh == ET - 1,
                     mark=(f"u{j}_{it}" if eh == ET - 1 else None), wait=wm)
            wsg = [f"g{j}_{it}"]
            if it >= 2:
                wsg.append(f"hv{j}_{it - 2}")
            elif j >= 1:
                wsg.append(f"hv{jm}_{IT - 2 + it}")
            else:
                wsg.append(f"dqg_dv{NIC - 1}")
            p.act("activation", sb["sgb"][:, par, :], pg[par][:, :], AF.Silu,
                  mark=f"sg{j}_{it}", wait=wsg)
            wb_ = it // 4
            whv = [f"u{j}_{it}", f"sg{j}_{it}"]
            if it % 4 == 0:
                if wb_ >= 2:
                    whv.append(f"hwr{j}_{wb_ - 2}")
                elif j >= 1:
                    whv.append(f"hwr{jm}_{IT // 4 - 2 + wb_}")
            p.dve("tensor_tensor", sb["hwr"][:, wb_ % 2, it % 4, :],
                  pu[par][:, :], sb["sgb"][:, par, :], ALU.mult,
                  mark=f"hv{j}_{it}", wait=whv)
            if it % 4 == 3:
                p.dma("gp", f"hwr{wb_ % 2}",
                      d["h_s"][it - 3:it + 1].transpose([1, 0, 2]),
                      sb["hwr"][:, wb_ % 2, :, :],
                      mark=f"hwr{j}_{wb_}", wait=(f"hv{j}_{it}",))
                if j >= 1:
                    if wb_ < 3:
                        load_h_block(j, wb_)
                    if wb_ < 2:
                        load_wd_block(j, wb_)

        p.scope = "down"
        if j + 1 < NCH:
            ln_apply(j + 1, j)
            load_wib_pair(j + 1, 0)
            load_wib_pair(j + 1, 1)
        if j == 0:
            drip(dq_tasks_u, 99)
            drip(dq_tasks_d, 99)
            for fb0 in range(3):
                load_h_block(0, fb0)
            for fb0 in range(2):
                load_wd_block(0, fb0)
        NB = 16
        qall = quarters + quarters2
        for g in range(2):
            for pp4 in range(2):
                pp = 4 * g + pp4
                if pp >= 2:
                    wx = [f"fin{j}p{pp - 2}"]
                elif j >= 1:
                    wx = [f"fin{jm}p{6 + pp}"]
                else:
                    wx = ["dqg_st62", "dqg_st63"]
                p.dma("gp", f"x2{pp % 2}", sb["x2b"][:, pp % 2, :, :],
                      d["xT"][256 * pp:256 * (pp + 1), j * C:(j + 1) * C]
                      .rearrange("(a el) t -> el a t", el=128),
                      mark=f"x2{j}p{pp}", wait=wx)
            for b in range(NB):
                fb = 16 * g + b
                slh = (j * 32 + fb) % 3
                sld = (j * 32 + fb) % 2
                for itl in range(4):
                    it = 4 * b + itl
                    for tt in range(8):
                        wm = []
                        if itl == 0 and tt == 0:
                            wm = [f"hld{j}f{fb}", f"wldA{j}f{fb}",
                                  f"wldB{j}f{fb}"]
                            if b == 0:
                                if g == 0:
                                    wm += [f"exp{j}h{H - 1}p2",
                                           f"exp{j}h{H - 1}p3",
                                           f"rawn{j}h{H - 1}",
                                           f"dn_ln{j}h{H - 1}",
                                           f"sg{j}_{IT - 1}",
                                           f"hv{j}_{IT - 1}"]
                                    if p7c[0] is not None:
                                        wm.append(p7c[0])
                                else:
                                    wm.append(f"gsil{j}g0")
                        wsrc = wdAbufs[sld] if tt < 4 else wdBbufs[sld]
                        p.mm(qall[tt],
                             wsrc[:, itl,
                                  (tt % 4) * 128:(tt % 4) * 128 + 128],
                             hrdbufs[slh][:, itl, :],
                             it == 0, it == IT - 1,
                             mark=(f"dnb{j}g{g}b{b}"
                                   if (itl == 3 and tt == 7) else None),
                             wait=wm)
                if fb + 3 < 32:
                    load_h_block(j, fb + 3)
                if fb + 2 < 32:
                    load_wd_block(j, fb + 2)
            for half in range(2):
                for tl in range(4):
                    tt = 4 * half + tl
                    wv = [f"dnb{j}g{g}b{NB - 1}"]
                    if tl == 0:
                        if half == 1:
                            wv.append(f"fin{j}p{4 * g + 1}")
                        elif g >= 1:
                            wv.append(f"fin{j}p3")
                        elif j >= 1:
                            wv.append(f"fin{jm}p7")
                    p.act("activation", sb["gsb"][:, tl, :], qall[tt],
                          AF.Silu,
                          mark=(f"gsil{j}g{g}h{half}"
                                if tl == 3 else None), wait=wv)
                for pq in range(2):
                    pp4 = 2 * half + pq
                    pp = 4 * g + pp4
                    f0 = 2 * pp
                    if pp4 >= 2:
                        p.dma("gp", f"x2{pp % 2}",
                              sb["x2b"][:, pp % 2, :, :],
                              d["xT"][256 * pp:256 * (pp + 1),
                                      j * C:(j + 1) * C]
                              .rearrange("(a el) t -> el a t", el=128),
                              mark=f"x2{j}p{pp}", wait=[f"fin{j}p{pp - 2}"])
                    wf = [f"gsil{j}g{g}h{half}", f"s_done{j}",
                          f"x2{j}p{pp}"]
                    if pp >= 2:
                        wf.append(f"outd{j}p{pp - 2}")
                    elif j >= 1:
                        wf.append(f"outd{jm}p{6 + pp}")
                    else:
                        wf += ["dqu_st62", "dqu_st63", f"dqu_dv{NIC - 1}"]
                    p.dve("tensor_tensor",
                          sb["tmpf"].ap().rearrange("p a t -> p (a t)"),
                          sb["s_sb"][:, f0:f0 + 2, :]
                          .rearrange("p a t -> p (a t)"),
                          sb["gsb"][:, 2 * pq:2 * pq + 2, :]
                          .rearrange("p a t -> p (a t)"),
                          ALU.mult, wait=wf)
                    p.dve("tensor_tensor",
                          sb["outb"][:, pp % 2, :, :]
                          .rearrange("p a t -> p (a t)"),
                          sb["tmpf"].ap().rearrange("p a t -> p (a t)"),
                          sb["x2b"][:, pp % 2, :, :]
                          .rearrange("p a t -> p (a t)"),
                          ALU.add, mark=f"fin{j}p{pp}", ss=True)
                    p.dma("gp", f"out{pp % 2}",
                          d["outT"][128 * f0:128 * (f0 + 2),
                                    j * C:(j + 1) * C]
                          .rearrange("(a el) t -> el a t", el=128),
                          sb["outb"][:, pp % 2, :, :],
                          mark=f"outd{j}p{pp}", wait=(f"fin{j}p{pp}",))
            p.mark(f"gsil{j}g{g}", "act")
            if g == 1:
                p7c[0] = f"gsil{j}g1"

    if p.cur == "sync":
        for nm in env.dma_streams:
            if p.cnt[nm] > 0:
                p.eng.wait_ge(env.sems[nm], p.cnt[nm])



_NC_CACHE = {}


def kernel(L0_attn_query_input, L0_attn_W_in, L0_attn_W_q, L0_attn_W_k,
           L0_attn_W_v, L0_attn_stored_patterns, L0_attn_norm_query_w,
           L0_attn_norm_retrieved_w, L0_attn_beta, L0_ffn_gate_w,
           L0_ffn_gate_scale, L0_ffn_up_w, L0_ffn_up_scale, L0_ffn_down_w,
           L0_ffn_down_scale):
    x = np.asarray(L0_attn_query_input, dtype=np.float32)
    B, S, E_ = x.shape
    xT = np.ascontiguousarray(x.reshape(B * S, E_).T)

    common = dict(
        w_inT=np.ascontiguousarray(np.asarray(L0_attn_W_in, np.float32).T),
        w_qT=np.ascontiguousarray(np.asarray(L0_attn_W_q, np.float32).T),
        w_kT=np.ascontiguousarray(np.asarray(L0_attn_W_k, np.float32).T),
        w_vT=np.ascontiguousarray(np.asarray(L0_attn_W_v, np.float32).T),
        spT=np.ascontiguousarray(
            np.asarray(L0_attn_stored_patterns, np.float32).transpose(0, 2, 1)),
        nqw=np.ascontiguousarray(
            np.asarray(L0_attn_norm_query_w, np.float32).reshape(ET, 128).T),
        nrw=np.ascontiguousarray(
            np.asarray(L0_attn_norm_retrieved_w, np.float32).reshape(ET, 128).T),
        beta=np.asarray(L0_attn_beta, np.float32).reshape(1),
        gwT=np.ascontiguousarray(np.asarray(L0_ffn_gate_w, np.int32).T),
        uwT=np.ascontiguousarray(np.asarray(L0_ffn_up_w, np.int32).T),
        dwT=np.ascontiguousarray(np.asarray(L0_ffn_down_w, np.int32).T),
        gsc=np.asarray(L0_ffn_gate_scale, np.float32).reshape(1, I),
        usc=np.asarray(L0_ffn_up_scale, np.float32).reshape(1, I),
        dsc=np.asarray(L0_ffn_down_scale, np.float32).reshape(1, E),
    )

    if "nc" not in _NC_CACHE:
        _NC_CACHE["nc"] = build_nc()
    nc = _NC_CACHE["nc"]

    in_maps = []
    for c in range(NCORE):
        m = dict(common)
        m["xT"] = np.ascontiguousarray(xT[:, c * NTOK:(c + 1) * NTOK])
        in_maps.append(m)

    res = run_bass_kernel_spmd(nc, in_maps, core_ids=list(range(NCORE)))
    out = np.empty((E, B * S), dtype=np.float32)
    for c in range(NCORE):
        out[:, c * NTOK:(c + 1) * NTOK] = res.results[c]["outT"]
    return out.T.reshape(B, S, E_)
